# version 08
# speedup vs baseline: 1.4972x; 1.0948x over previous
import numpy as np
import ml_dtypes
import concourse.bass as bass
import concourse.mybir as mybir
from concourse.bass_utils import run_bass_kernel_spmd

F32 = mybir.dt.float32
BF16 = mybir.dt.bfloat16
I32 = mybir.dt.int32
AF = mybir.ActivationFunctionType
ALU = mybir.AluOpType
AX = mybir.AxisListType
BF = ml_dtypes.bfloat16

NCORE = 8
T = 1024
D = 4096
FC = D // 128
NT = T // 128

ENGINES = ("t", "v", "s", "g", "q")


class Sem:
    def __init__(self, nc, name):
        self.h = nc.alloc_semaphore(name)
        self.n = 0
        self.name = name

    def bump(self, k=1):
        self.n += k
        return self.n


class B:

    def __init__(self, nc):
        self.nc = nc
        self.q = {e: [] for e in ENGINES}
        self._sems = {}
        self._last_wait = {e: {} for e in ENGINES}

    def sem(self, name):
        if name not in self._sems:
            self._sems[name] = Sem(self.nc, name)
        return self._sems[name]

    def wait(self, eng, sem, val):
        if val is None:
            return
        lw = self._last_wait[eng]
        if lw.get(sem.name, -1) >= val:
            return
        lw[sem.name] = val
        self.q[eng].append(lambda h, s=sem.h, v=val: h.wait_ge(s, v))

    AUTO = {"v": "vsem", "s": "ssem", "g": "gsem"}

    def op(self, eng, fn, deps=(), inc=None, inck=1):
        auto = self.AUTO.get(eng) if inck != 16 else None
        if auto is not None:
            asem = self.sem(auto)
            if inc is not None and inc is not asem:
                raise ValueError(f"engine {eng} must use its auto sem {auto}")
            self.wait(eng, asem, asem.n)
            inc = asem
            inck = 1
        for (sem, val) in deps:
            self.wait(eng, sem, val)
        ms = inc.bump(inck) if inc is not None else None
        def g(h, fn=fn, inc=inc, inck=inck):
            i = fn(h)
            if inc is not None:
                i.then_inc(inc.h, inck)
        self.q[eng].append(g)
        return ms

    def raw(self, eng, fn):
        self.q[eng].append(lambda h, fn=fn: fn(h))

    def fence(self):
        for e in ENGINES:
            for s in list(self._sems.values()):
                if s.n > 0:
                    self.wait(e, s, s.n)

    def dma(self, eng, out, in_, deps=(), inc=None, **kw):
        return self.op(eng, lambda h: h.dma_start(out=out, in_=in_, **kw),
                       deps=deps, inc=inc, inck=16)

    def finalize(self):
        nc = self.nc
        with nc.Block() as block:
            for e, dec in (("q", block.sync), ("g", block.gpsimd), ("t", block.tensor),
                           ("v", block.vector), ("s", block.scalar)):
                fns = self.q[e]
                if not fns:
                    continue
                def body(h, fns=fns):
                    for f in fns:
                        f(h)
                dec(body)
        return nc


class Ring:

    def __init__(self, b, name, nslots):
        self.b = b
        self.sems = [b.sem(f"{name}{i}") for i in range(nslots)]
        self.n = nslots

    def dma(self, i, eng, out, in_, deps=(), **kw):
        s = self.sems[i % self.n]
        ms = self.b.dma(eng, out, in_, deps=deps, inc=s, **kw)
        return (s, ms)



def tile_lhsT(wT):
    K, M = wT.shape
    assert K % 128 == 0 and M % 128 == 0
    return np.ascontiguousarray(
        wT.reshape(K // 128, 128, M // 128, 128).transpose(2, 0, 1, 3))


def fm_of(x2d):
    return np.ascontiguousarray(x2d.T)


ORDER = np.concatenate([np.arange(0, D, 2), np.arange(1, D, 2)])
INV_ORDER = np.argsort(ORDER)



def alloc_fm(nc, name, dt, ntok=T, nfc=FC):
    return nc.alloc_sbuf_tensor(name, [128, nfc * ntok], dt).ap()


def fmc(ap, fc, ntok=T):
    return ap[:, fc * ntok:(fc + 1) * ntok]


def ln_fm(b, name, x_dram, xbf, consts, psA, psB, halfdve=True, upto=None, scratch_off=None):
    nc = b.nc
    dsem = b.sem("dsem"); tsem = b.sem("tsem"); vsem = b.sem("vsem"); ssem = b.sem("ssem")
    if scratch_off is not None:
        xst = nc.alloc_sbuf_tensor_at(name + "_xst", [128, 2 * T], F32, offset=scratch_off).ap()
    else:
        xst = nc.alloc_sbuf_tensor(name + "_xst", [128, 2 * T], F32).ap()
    xring = Ring(b, "lnxr", 2)
    casts = {}
    for fc in range(FC):
        deps = [(vsem, casts[fc - 2])] if fc - 2 in casts else []
        lf = xring.dma(fc, "q", xst[:, (fc % 2) * T:(fc % 2 + 1) * T],
                       x_dram[fc * 128:(fc + 1) * 128, :], deps=deps)
        casts[fc] = b.op("v", lambda h, fc=fc: h.tensor_copy(
            fmc(xbf, fc), xst[:, (fc % 2) * T:(fc % 2 + 1) * T]),
            deps=[lf], inc=vsem)
    ld_v = casts[FC - 1]
    if upto == "cast":
        return ("dbg", [fmc(xbf, fc) for fc in range(4)])
    if scratch_off is not None:
        sqr = nc.alloc_sbuf_tensor_at(name + "_sqr", [128, 2 * T], BF16, offset=scratch_off + 16384).ap()
    else:
        sqr = nc.alloc_sbuf_tensor(name + "_sqr", [128, 2 * T], BF16).ap()
    tS = tQ = None
    sqms = {}
    qms = {}
    for fc in range(FC):
        sdeps = [(vsem, casts[fc])]
        if fc - 2 in qms:
            sdeps.append((tsem, qms[fc - 2]))
        sqms[fc] = b.op("s", lambda h, fc=fc: h.activation(
            sqr[:, (fc % 2) * T:(fc % 2 + 1) * T], fmc(xbf, fc), AF.Square),
            deps=sdeps, inc=ssem)
        for half in range(2):
            last = (fc == FC - 1 and half == 1)
            tS = b.op("t", lambda h, fc=fc, half=half: h.matmul(
                psA[0:1, half * 512:(half + 1) * 512],
                consts["ones128"], xbf[:, fc * T + half * 512:fc * T + (half + 1) * 512],
                start=(fc == 0), stop=(fc == FC - 1)),
                deps=[(vsem, casts[fc])] + ([consts["ready"]] if fc == 0 else []),
                inc=(tsem if last else None)) or tS
        for half in range(2):
            m = b.op("t", lambda h, fc=fc, half=half: h.matmul(
                psB[0:1, half * 512:(half + 1) * 512],
                consts["ones128"], sqr[:, (fc % 2) * T + half * 512:(fc % 2) * T + (half + 1) * 512],
                start=(fc == 0), stop=(fc == FC - 1)),
                deps=[(ssem, sqms[fc])],
                inc=(tsem if half == 1 else None))
            if half == 1:
                qms[fc] = m
                tQ = m
    if upto == "stats":
        sdump = nc.alloc_sbuf_tensor(name + "_sd", [1, 2 * T], F32).ap()
        b.op("v", lambda h: h.tensor_copy(sdump[:, 0:T], psA[0:1, :]), deps=[(tsem, tS)])
        b.op("v", lambda h: h.tensor_copy(sdump[:, T:2 * T], psB[0:1, :]), deps=[(tsem, tQ)])
        return ("dbg", [sdump[:, 0:T], sdump[:, T:2 * T]])
    if scratch_off is not None:
        st = nc.alloc_sbuf_tensor_at(name + "_st", [1, 4 * T], F32, offset=scratch_off + 20480).ap()
    else:
        st = nc.alloc_sbuf_tensor(name + "_st", [1, 4 * T], F32).ap()
    mean, var, rstd, negmr = (st[:, i * T:(i + 1) * T] for i in range(4))
    msq = negmr
    if scratch_off is not None:
        rows_bf = nc.alloc_sbuf_tensor_at(name + "_rows", [1, 2 * T], BF16, offset=scratch_off + 36864).ap()
    else:
        rows_bf = nc.alloc_sbuf_tensor(name + "_rows", [1, 2 * T], BF16).ap()
    rstd_bf = rows_bf[:, 0:T]; negmr_bf = rows_bf[:, T:2 * T]
    b.op("v", lambda h: h.tensor_scalar(mean, psA[0:1, :], 1.0 / D, None, op0=ALU.mult),
         deps=[(tsem, tS)])
    b.op("v", lambda h: h.tensor_scalar(var, psB[0:1, :], 1.0 / D, None, op0=ALU.mult),
         deps=[(tsem, tQ)])
    b.op("v", lambda h: h.tensor_tensor(msq, mean, mean, op=ALU.mult))
    vv = b.op("v", lambda h: h.scalar_tensor_tensor(var, var, 1e-5, msq, op0=ALU.add, op1=ALU.subtract), inc=vsem)
    lg = b.op("s", lambda h: h.activation(var, var, AF.Ln),
              deps=[(vsem, vv)], inc=ssem)
    ex = b.op("s", lambda h: h.activation(rstd, var, AF.Exp, scale=-0.5), inc=ssem)
    b.op("v", lambda h: h.scalar_tensor_tensor(negmr, mean, -1.0, rstd, op0=ALU.mult, op1=ALU.mult),
         deps=[(ssem, ex)])
    b.op("v", lambda h: h.tensor_copy(rstd_bf, rstd))
    vr = b.op("v", lambda h: h.tensor_copy(negmr_bf, negmr), inc=vsem)
    if upto == "rows":
        return ("dbg", [rstd, negmr])
    tB = None
    for half in range(2):
        b.op("t", lambda h, half=half: h.matmul(
            psA[:, half * 512:(half + 1) * 512], consts["ones1"],
            rstd_bf[:, half * 512:(half + 1) * 512], start=True, stop=True),
            deps=[(vsem, vr)])
        tB = b.op("t", lambda h, half=half: h.matmul(
            psB[:, half * 512:(half + 1) * 512], consts["ones1"],
            negmr_bf[:, half * 512:(half + 1) * 512], start=True, stop=True),
            inc=(tsem if half == 1 else None)) or tB
    if scratch_off is not None:
        Brow = nc.alloc_sbuf_tensor_at(name + "_bc", [128, 2 * T], BF16, offset=scratch_off + 40960).ap()
    else:
        Brow = nc.alloc_sbuf_tensor(name + "_bc", [128, 2 * T], BF16).ap()
    Bs = Brow[:, 0:T]; Bm = Brow[:, T:2 * T]
    b.op("v", lambda h: h.tensor_copy(Bs, psA[:, :]), deps=[(tsem, tB)])
    ve = b.op("v", lambda h: h.tensor_copy(Bm, psB[:, :]), inc=vsem)
    gsem = b.sem("gsem")
    last_v = last_g = None
    for half2 in range(2):
        sl = slice(half2 * 16 * T, (half2 + 1) * 16 * T)
        eng = "v" if half2 == 0 or not halfdve else "g"
        m1_ = b.op(eng, lambda h, sl=sl: h.tensor_tensor(
            xbf[:, sl], xbf[:, sl],
            Bs.rearrange("p (o t) -> p o t", o=1).broadcast_to((128, 16, T)),
            op=ALU.mult), deps=[(vsem, ve)])
        m2_ = b.op(eng, lambda h, sl=sl: h.tensor_tensor(
            xbf[:, sl], xbf[:, sl],
            Bm.rearrange("p (o t) -> p o t", o=1).broadcast_to((128, 16, T)),
            op=ALU.add))
        if eng == "v":
            last_v = m2_
        else:
            last_g = m2_
    if last_g is not None:
        b.wait("v", b.sem("gsem"), last_g)
    done = b.op("v", lambda h: h.nop(), deps=[(vsem, last_v)] if last_v else (), inc=vsem)
    return done


def dbg_store(b, x_out, srcs, tag):
    nc = b.nc
    vsem = b.sem("vsem"); osem = b.sem("osem")
    stage = nc.alloc_sbuf_tensor(tag + "dbgstage", [128, 2 * 1024], F32).ap()
    row = 0
    prev_st = None
    for i, ap in enumerate(srcs):
        p, f = ap.shape
        t32 = stage[0:p, (i % 2) * 1024:(i % 2) * 1024 + f]
        deps = [(osem, prev_st)] if prev_st is not None else []
        vm = b.op("v", lambda h, t32=t32, ap=ap: h.tensor_copy(t32, ap), deps=deps, inc=vsem)
        prev_st = b.dma("q", x_out[row:row + p, 0:f], t32, deps=[(vsem, vm)], inc=osem)
        row += p
    b.wait("q", osem, osem.n)


def build_l0_ffn(b, x_in, x_out, inp, tag, upto=None):
    nc = b.nc
    dsem = b.sem("dsem"); tsem = b.sem("tsem"); vsem = b.sem("vsem")
    ssem = b.sem("ssem"); osem = b.sem("osem")

    consts = inp["consts"]
    psA = nc.alloc_psum_tensor(tag + "psA", [128, 1024], F32).ap()
    psB = nc.alloc_psum_tensor(tag + "psB", [128, 1024], F32).ap()
    psC = nc.alloc_psum_tensor(tag + "psC", [128, 1024], F32).ap()
    psDg = nc.alloc_psum_tensor(tag + "psDg", [128, 512], F32).ap()
    psDm = nc.alloc_psum_tensor(tag + "psDm", [128, 512], F32).ap()

    xbf = alloc_fm(nc, tag + "xbf", BF16)
    ln_done = ln_fm(b, tag + "ln", x_in, xbf, consts, psA, psB,
                    upto=(upto if upto in ("cast", "stats", "rows") else None))
    if isinstance(ln_done, tuple):
        dbg_store(b, x_out, ln_done[1], tag)
        return {}
    if upto == "ln":
        dbg_store(b, x_out, [fmc(xbf, fc) for fc in range(FC)], tag)
        return {}

    p0 = nc.alloc_sbuf_tensor(tag + "p0", [128, FC * 68], BF16).ap()
    bwsb = nc.alloc_sbuf_tensor(tag + "bw", [64, FC * 128], BF16).ap()
    wstage = nc.alloc_sbuf_tensor(tag + "wstage", [128, 4096], F32).ap()
    wring = Ring(b, "wr", 2)
    lf = wring.dma(0, "q", wstage[:, 0:FC * 68].rearrange("p (c m) -> p c m", c=FC),
                   inp["p0w"].rearrange("c p m -> p c m"), deps=[])
    p0_ready = b.op("v", lambda h: h.tensor_copy(p0, wstage[:, 0:FC * 68]), deps=[lf])
    lf2 = wring.dma(1, "q", wstage[0:64, :].rearrange("p (c m) -> p c m", c=FC),
                    inp["bw"].rearrange("c p m -> p c m"), deps=[(b.sem("vsem"), p0_ready)])
    w_ready = b.op("v", lambda h: h.tensor_copy(bwsb, wstage[0:64, :]), deps=[lf2])

    t_mid = None
    for fc in range(FC):
        for half in range(2):
            last = (fc == FC - 1 and half == 1)
            t_mid = b.op("t", lambda h, fc=fc, half=half: h.matmul(
                psC[0:68, half * 512:(half + 1) * 512],
                p0[:, fc * 68:(fc + 1) * 68],
                xbf[:, fc * T + half * 512:fc * T + (half + 1) * 512],
                start=(fc == 0), stop=(fc == FC - 1)),
                deps=[(vsem, ln_done), (vsem, p0_ready)],
                inc=(tsem if last else None)) or t_mid
    sb_ml = nc.alloc_sbuf_tensor(tag + "ml", [128, 1024], F32).ap()
    v_ml = b.op("v", lambda h: h.tensor_copy(sb_ml[0:68, :], psC[0:68, :]),
                deps=[(tsem, t_mid)], inc=vsem)
    if upto == "mid":
        dbg_store(b, x_out, [sb_ml[0:68, :]], tag)
        return {}

    scr = nc.alloc_sbuf_tensor(tag + "scr", [128, 64], F32).ap()
    g8 = scr[:, 0:8]; m1c = scr[:, 8:9]; m2c = scr[:, 9:10]
    ge4 = scr[:, 10:14]; mk4 = scr[:, 28:32]; negm1 = scr[:, 16:17]
    keep = scr[:, 17:21]; e4 = scr[:, 21:25]; z = scr[:, 25:26]; rz = scr[:, 26:27]
    gates = nc.alloc_sbuf_tensor(tag + "gates", [128, 32], F32).ap()
    mts = nc.alloc_sbuf_tensor(tag + "mts", [128, 64], F32).ap()
    mids = nc.alloc_sbuf_tensor(tag + "mids", [64, 1024], BF16).ap()
    b.op("v", lambda h: h.memset(g8, -1e30))
    v_prev_copy = None
    for n in range(NT):
        sl = slice(n * 128, (n + 1) * 128)
        deps = [(vsem, v_ml)]
        if v_prev_copy is not None:
            deps.append((vsem, v_prev_copy))
        t_g = b.op("t", lambda h, sl=sl: h.transpose(
            psDg[:, 0:4], sb_ml[64:68, sl], inp["identF64"][64:68, 0:4]), deps=deps, inc=tsem)
        cg = b.op("v", lambda h: h.tensor_copy(g8[:, 0:4], psDg[:, 0:4]), deps=[(tsem, t_g)], inc=vsem)
        b.wait("v", vsem, cg)
        b.op("v", lambda h: h.tensor_reduce(m1c, g8[:, 0:4], axis=AX.X, op=ALU.max))
        b.op("v", lambda h: h.tensor_scalar(ge4, g8[:, 0:4], m1c, None, op0=ALU.is_ge))
        b.op("v", lambda h: h.scalar_tensor_tensor(mk4, ge4, -1e30, g8[:, 0:4], op0=ALU.mult, op1=ALU.add))
        b.op("v", lambda h: h.tensor_reduce(m2c, mk4, axis=AX.X, op=ALU.max))
        v_m1 = b.op("v", lambda h: h.tensor_scalar(negm1, m1c, -1.0, None, op0=ALU.mult), inc=vsem)
        s_e = b.op("s", lambda h: h.activation(e4, g8[:, 0:4], AF.Exp, bias=negm1),
                   deps=[(vsem, v_m1)], inc=ssem)
        b.op("v", lambda h: h.tensor_scalar(keep, g8[:, 0:4], m2c, None, op0=ALU.is_ge),
             deps=[(ssem, s_e)])
        b.op("v", lambda h: h.tensor_tensor(e4, e4, keep, op=ALU.mult))
        b.op("v", lambda h: h.tensor_reduce(z, e4, axis=AX.X, op=ALU.add))
        b.op("v", lambda h: h.reciprocal(rz, z))
        v_g = b.op("v", lambda h, n=n: h.tensor_scalar(
            gates[:, n * 4:(n + 1) * 4], e4, rz, 2.0, op0=ALU.mult, op1=ALU.mult), inc=vsem)
        t_m = b.op("t", lambda h, sl=sl: h.transpose(
            psDm[:, 0:64], sb_ml[0:64, sl], inp["identF"][0:64, 0:64]), deps=[(vsem, v_g)], inc=tsem)
        v_mts = None
        for e in range(4):
            v_mts = b.op("v", lambda h, e=e, n=n: h.tensor_scalar(
                mts[:, e * 16:(e + 1) * 16], psDm[:, e * 16:(e + 1) * 16],
                gates[:, n * 4 + e:n * 4 + e + 1], None, op0=ALU.mult),
                deps=[(tsem, t_m)], inc=(vsem if e == 3 else None)) or v_mts
        t_b2 = b.op("t", lambda h: h.transpose(
            psA[0:64, 0:128], mts, inp["identF"]), deps=[(vsem, v_mts)], inc=tsem)
        v_prev_copy = b.op("v", lambda h, sl=sl: h.tensor_copy(mids[:, sl], psA[0:64, 0:128]),
                           deps=[(tsem, t_b2)], inc=vsem)
        if upto == "gate0" and n == 0:
            dbg0 = nc.alloc_sbuf_tensor(tag + "dbg0", [128, 64], F32).ap()
            b.op("v", lambda h: h.tensor_copy(dbg0, scr))
            dbg_store(b, x_out, [dbg0, gates, mts], tag)
            return {}

    if upto == "gate":
        dbg_store(b, x_out, [gates, mids], tag)
        return {}
    xs = nc.alloc_sbuf_tensor(tag + "xs", [128, 2 * 1024], F32).ap()
    yo = nc.alloc_sbuf_tensor(tag + "yo", [128, 2 * 1024], F32).ap()
    xr = Ring(b, "xl", 2)
    sr = Ring(b, "so", 2)
    ld_x = {}
    ev = {}
    st = {}
    for oc in range(FC):
        deps = [(vsem, ev[oc - 2])] if oc - 2 in ev else []
        ld_x[oc] = xr.dma(oc, "q", xs[:, (oc % 2) * 1024:(oc % 2 + 1) * 1024],
                          x_in[oc * 128:(oc + 1) * 128, :], deps=deps)
        ps = psC if oc % 2 == 0 else psB
        mmdeps = [(vsem, v_prev_copy), (vsem, w_ready)]
        if oc - 2 in ev:
            mmdeps.append((vsem, ev[oc - 2]))
        for half in range(2):
            t_d = b.op("t", lambda h, oc=oc, half=half, ps=ps: h.matmul(
                ps[:, half * 512:(half + 1) * 512],
                bwsb[:, oc * 128:(oc + 1) * 128],
                mids[:, half * 512:(half + 1) * 512],
                start=True, stop=True), deps=mmdeps, inc=(tsem if half == 1 else None))
        vdeps = [(tsem, t_d), ld_x[oc]]
        if oc - 2 in st:
            vdeps.append(st[oc - 2])
        ev[oc] = b.op("v", lambda h, oc=oc, ps=ps: h.tensor_tensor(
            yo[:, (oc % 2) * 1024:(oc % 2 + 1) * 1024], ps[:, :],
            xs[:, (oc % 2) * 1024:(oc % 2 + 1) * 1024], op=ALU.add),
            deps=vdeps, inc=vsem)
        st[oc] = sr.dma(oc, "q", x_out[oc * 128:(oc + 1) * 128, :],
                        yo[:, (oc % 2) * 1024:(oc % 2 + 1) * 1024],
                        deps=[(vsem, ev[oc])])
    for s in sr.sems:
        b.wait("q", s, s.n)
    return st


def host_l0_ffn_inputs(A, Bm, Wr):
    AT = np.ascontiguousarray(A.reshape(64, 4096).T)
    WrT = np.ascontiguousarray(Wr.T)
    P0 = np.concatenate([AT, WrT], axis=1)
    p0w = np.ascontiguousarray(P0.reshape(FC, 128, 68))
    Bst = np.ascontiguousarray(Bm.transpose(0, 2, 1).reshape(64, 4096))
    bw = np.ascontiguousarray(Bst.reshape(64, FC, 128).transpose(1, 0, 2))
    return p0w.astype(np.float32), bw.astype(np.float32)


def make_consts(nc):
    ones128 = nc.dram_tensor("c_ones128", [128, 1], BF16, kind="ExternalInput").ap()
    ones1 = nc.dram_tensor("c_ones1", [1, 128], BF16, kind="ExternalInput").ap()
    identF = nc.dram_tensor("c_identF", [128, 128], F32, kind="ExternalInput").ap()
    identB = nc.dram_tensor("c_identB", [128, 128], BF16, kind="ExternalInput").ap()
    identF64 = nc.dram_tensor("c_identF64", [128, 128], F32, kind="ExternalInput").ap()
    identF32 = nc.dram_tensor("c_identF32", [128, 128], F32, kind="ExternalInput").ap()
    identB32 = nc.dram_tensor("c_identB32", [128, 128], BF16, kind="ExternalInput").ap()
    return dict(ones128=ones128, ones1=ones1, identF=identF, identB=identB,
                identF64=identF64, identF32=identF32, identB32=identB32)


def load_consts(b, consts_dram):
    nc = b.nc
    dsem = b.sem("csem")
    sb = {}
    sb["ones128"] = nc.alloc_sbuf_tensor("sb_ones128", [128, 1], BF16).ap()
    sb["ones1"] = nc.alloc_sbuf_tensor("sb_ones1", [1, 128], BF16).ap()
    sb["identF"] = nc.alloc_sbuf_tensor("sb_identF", [128, 128], F32).ap()
    sb["identB"] = nc.alloc_sbuf_tensor("sb_identB", [128, 128], BF16).ap()
    sb["identF64"] = nc.alloc_sbuf_tensor("sb_identF64", [128, 128], F32).ap()
    sb["identF32"] = nc.alloc_sbuf_tensor("sb_identF32", [128, 128], F32).ap()
    sb["identB32"] = nc.alloc_sbuf_tensor("sb_identB32", [128, 128], BF16).ap()
    ms = None
    for k in list(sb.keys()):
        ms = b.dma("q", sb[k], consts_dram[k], inc=dsem)
    sb["ready"] = (dsem, ms)
    return sb, ms


def host_consts():
    return {
        "c_ones128": np.ones((128, 1), BF),
        "c_ones1": np.ones((1, 128), BF),
        "c_identF": np.eye(128, dtype=np.float32),
        "c_identB": np.eye(128, dtype=BF),
        "c_identF64": np.eye(128, 128, k=-64).astype(np.float32),
        "c_identF32": np.eye(128, 128, k=-32).astype(np.float32),
        "c_identB32": np.eye(128, 128, k=-32).astype(BF),
    }


def build_l1_ffn(b, x_in, x_out, inp, tag, mode):
    nc = b.nc
    dsem = b.sem("dsem"); tsem = b.sem("tsem"); vsem = b.sem("vsem")
    gsem = b.sem("gsem"); osem = b.sem("osem"); xsem = b.sem("xsem")
    wsem = b.sem("wsem")
    consts = inp["consts"]
    import os
    dumpmode = os.environ.get("DUMPXN")

    psA = nc.alloc_psum_tensor(tag + "psA", [128, 1024], F32).ap()
    psB = nc.alloc_psum_tensor(tag + "psB", [128, 1024], F32).ap()
    psC = nc.alloc_psum_tensor(tag + "psC", [128, 1024], F32).ap()

    xbf = alloc_fm(nc, tag + "xbf", BF16)
    ln_done = ln_fm(b, tag + "ln", x_in, xbf, consts, psA, psB)

    dq_done = None
    if mode == "build":
        sclr = nc.alloc_sbuf_tensor(tag + "sclr", [1, 4096], F32).ap()
        sclb = nc.alloc_sbuf_tensor(tag + "sclb", [1, 4096], BF16).ap()
        sbc = nc.alloc_sbuf_tensor(tag + "sbc", [128, 4096], BF16).ap()
        import os as _os
        l1s = b.sem("sc")
        lds = b.dma("q", sclr, inp["scales_r"], inc=l1s)
        c1 = b.op("v", lambda h: h.tensor_copy(sclb, sclr), deps=[(l1s, 16), consts["ready"]])
        el = None
        sb_ms = None
        sb_cp = {}
        for hh in range(8 if not el else 0):
            mdeps = [(vsem, c1)]
            if hh - 2 in sb_cp:
                mdeps.append((vsem, sb_cp[hh - 2]))
            sb_ms = b.op("t", lambda h, hh=hh: h.matmul(
                psC[:, (hh % 2) * 512:(hh % 2 + 1) * 512], consts["ones1"],
                sclb[:, hh * 512:(hh + 1) * 512], start=True, stop=True),
                deps=mdeps, inc=tsem)
            sb_cp[hh] = b.op("v", lambda h, hh=hh: h.tensor_copy(
                sbc[:, hh * 512:(hh + 1) * 512], psC[:, (hh % 2) * 512:(hh % 2 + 1) * 512]),
                deps=[(tsem, sb_ms)])
        if not el:
            sb_done = b.op("v", lambda h: h.tensor_copy(sclb, sclb))

        pkb = nc.alloc_sbuf_tensor(tag + "pk", [128, 2048], I32).ap()
        orb = nc.alloc_sbuf_tensor(tag + "or", [128, 2048], F32).ap()
        ii = nc.alloc_sbuf_tensor(tag + "ii", [128, 2048], I32).ap()
        wf = nc.alloc_sbuf_tensor(tag + "wf", [128, 2048], F32).ap()
        wb = nc.alloc_sbuf_tensor(tag + "wb", [128, 4096], BF16).ap()
        wstms = {}
        pksem = b.sem("pks"); orsem = b.sem("ors")
        wstsem = [b.sem("ws0"), b.sem("ws1")]

        def dq_iter(k, oh):
            osl = slice(oh * 2048, (oh + 1) * 2048)
            pdep = [(vsem, wstms["prev_vi"])] if "prev_vi" in wstms else []
            ldp = b.dma("q", pkb, inp["pkt"][k * 128:(k + 1) * 128, osl],
                        deps=pdep, inc=pksem)
            for half, ic in ((0, k), (1, 16 + k)):
                wbsl = wb[:, half * 2048:(half + 1) * 2048]
                ldo = b.dma("q", orb, inp["orthot"][ic * 128:(ic + 1) * 128, osl],
                            deps=[(gsem, wstms["gw"])] if "gw" in wstms else [],
                            inc=orsem)
                if half == 0:
                    vi = b.op("v", lambda h: h.tensor_scalar(
                        ii, pkb, 15, None, op0=ALU.bitwise_and),
                        deps=[(pksem, ldp)])
                else:
                    vi = b.op("v", lambda h: h.tensor_scalar(
                        ii, pkb, 4, 15,
                        op0=ALU.logical_shift_right, op1=ALU.bitwise_and))
                wstms["prev_vi"] = vi
                b.op("v", lambda h: h.tensor_copy(wf, ii))
                b.op("v", lambda h: h.tensor_scalar(wf, wf, -8.0, None, op0=ALU.add),
                     deps=[(vsem, sb_done)])
                vm = b.op("v", lambda h, osl=osl: h.tensor_tensor(
                    wf, wf, sbc[:, osl], op=ALU.mult))
                gdeps = [(vsem, vm), (orsem, ldo)]
                if ("st", half) in wstms:
                    gdeps.append((wstsem[half], wstms[("st", half)]))
                gw = b.op("g", lambda h, wbsl=wbsl: h.tensor_tensor(
                    wbsl, wf, orb, op=ALU.add), deps=gdeps)
                wstms["gw"] = gw
                wstms[("st", half)] = b.dma(
                    "g",
                    inp["wt"][oh * 16:(oh + 1) * 16, ic, :, :].rearrange("oc p o -> p oc o"),
                    wbsl.rearrange("p (oc o) -> p oc o", oc=16),
                    deps=[(gsem, gw)], inc=wstsem[half])

        for k in range(16):
            dq_iter(k, 0)
        dq_half = [(wstsem[0], wstsem[0].n), (wstsem[1], wstsem[1].n)]
        dq_done = None

    wt2 = [nc.alloc_sbuf_tensor(tag + "wtA", [128, 4096], BF16).ap(),
           nc.alloc_sbuf_tensor(tag + "wtB", [128, 4096], BF16).ap()]
    xs = nc.alloc_sbuf_tensor(tag + "xs", [128, 2 * 1024], F32).ap()
    yo = nc.alloc_sbuf_tensor(tag + "yo", [128, 2 * 1024], F32).ap()

    wlr = Ring(b, "wl", 2)
    xlr = Ring(b, "xl", 2)
    slr = Ring(b, "st", 2)
    ld_w = {}; ev = {}; st = {}; mmend = {}
    for oc in range(FC):
        if mode == "build":
            if oc < 16:
                dq_iter(oc, 1)
            if oc == 16:
                dq_done = [(wstsem[0], wstsem[0].n), (wstsem[1], wstsem[1].n)]
        par = oc % 2
        wsl = wt2[par]
        wdeps = [(tsem, mmend[oc - 2])] if oc - 2 in mmend else []
        if mode == "build":
            wdeps.extend(dq_half if oc < 16 else dq_done)
        ld_w[oc] = wlr.dma(oc, "q", wsl.rearrange("p (ic o) -> p ic o", ic=FC),
                           inp["wt"][oc, :, :, :].rearrange("ic p o -> p ic o"),
                           deps=wdeps)
        xdeps2 = [(vsem, ev[oc - 2])] if oc - 2 in ev else []
        ldx = xlr.dma(oc, "q", xs[:, par * 1024:(par + 1) * 1024],
                      x_in[oc * 128:(oc + 1) * 128, :], deps=xdeps2)
        ps = psC if par == 0 else psB
        mmdeps = [ld_w[oc], (vsem, ln_done)]
        if oc - 2 in ev:
            mmdeps.append((vsem, ev[oc - 2]))
        last = None
        for ic in range(FC):
            for half in range(2):
                last = b.op("t", lambda h, ic=ic, half=half, ps=ps, wsl=wsl: h.matmul(
                    ps[:, half * 512:(half + 1) * 512],
                    wsl[:, ic * 128:(ic + 1) * 128],
                    xbf[:, ic * T + half * 512:ic * T + (half + 1) * 512],
                    start=(ic == 0), stop=(ic == FC - 1)),
                    deps=mmdeps if ic == 0 else (),
                    inc=(tsem if (ic == FC - 1 and half == 1) else None))
        mmend[oc] = last
        ev[oc] = b.op("v", lambda h, par=par, ps=ps: h.scalar_tensor_tensor(
            yo[:, par * 1024:(par + 1) * 1024], ps[:, :], 0.5,
            xs[:, par * 1024:(par + 1) * 1024], op0=ALU.mult, op1=ALU.add),
            deps=[(tsem, mmend[oc]), ldx] + ([st[oc - 2]] if oc - 2 in st else []))
        st[oc] = slr.dma(oc, "q", x_out[oc * 128:(oc + 1) * 128, :],
                         yo[:, par * 1024:(par + 1) * 1024], deps=[(vsem, ev[oc])])
    if dumpmode == "end" and inp.get("xn_out") is not None:
        xn_out = inp["xn_out"]
        stg = yo[:, 0:T]
        dr = Ring(b, "dr", 1)
        pv = None
        for fc in range(FC):
            deps = [(vsem, ev[FC - 1]), st[FC - 1], st[FC - 2]] + ([pv] if pv else [])
            cm = b.op("v", lambda h, fc=fc: h.tensor_copy(stg, fmc(xbf, fc)), deps=deps)
            pv = dr.dma(0, "q", xn_out[fc * 128:(fc + 1) * 128, :], stg, deps=[(vsem, cm)])
    for s in slr.sems:
        b.wait("q", s, s.n)
    if mode == "build":
        for s in wstsem:
            b.wait("q", s, s.n)
    return st




def host_l0_attn_inputs(Wqk_w, Wqk_b, Wv_w, Wv_b, Wo_w, Wo_b):
    H, E = 8, 64
    rows_q = np.array([h * 128 + 2 * e for h in range(H) for e in range(E)])
    rows_k = rows_q + 1
    WQ = Wqk_w[rows_q]
    WK = Wqk_w[rows_k]
    stack = np.concatenate([WQ, WK, Wv_w], axis=0)
    stack = stack[:, ORDER]
    wqkv = tile_lhsT(np.ascontiguousarray(stack.T))
    bias = np.concatenate([Wqk_b[rows_q], Wqk_b[rows_k], Wv_b])
    bqkv = np.ascontiguousarray(bias.reshape(12, 128).T)
    WoP = Wo_w[ORDER, :]
    woT = np.ascontiguousarray(WoP.T)
    wo = np.ascontiguousarray(woT.reshape(8, 64, 32, 128).transpose(2, 0, 1, 3))
    wob = np.ascontiguousarray(Wo_b[ORDER].reshape(32, 128).T)
    return (wqkv.astype(np.float32), bqkv.astype(np.float32),
            wo.astype(np.float32), wob.astype(np.float32))


def build_l0_attn(b, x_in, x_out, inp, tag, core_id_dummy=None):
    nc = b.nc
    tsem = b.sem("tsem"); vsem = b.sem("vsem"); ssem = b.sem("ssem")
    gsem = b.sem("gsem"); csem2 = b.sem("cc")
    consts = inp["consts"]
    H, E = 8, 64

    psA = nc.alloc_psum_tensor(tag + "psA", [128, 1024], F32).ap()
    psB = nc.alloc_psum_tensor(tag + "psB", [128, 1024], F32).ap()
    psS = nc.alloc_psum_tensor(tag + "psS", [128, 1024], F32).ap()
    psAV = nc.alloc_psum_tensor(tag + "psAV", [128, 1024], F32).ap()
    psT = psS.tensor.bitcast(BF16).ap()[:, 0:128]

    arena_off = nc.sbuf_base
    xbf = alloc_fm(nc, tag + "xbf", BF16)
    ln_done = ln_fm(b, tag + "ln", x_in, xbf, consts, psA, psB)

    bq = nc.alloc_sbuf_tensor(tag + "bq", [128, 12], F32).ap()
    bo = nc.alloc_sbuf_tensor(tag + "bo", [128, 32], F32).ap()
    bsem = b.sem("bs")
    b.dma("q", bq, inp["bqkv"], inc=bsem)
    b.dma("q", bo, inp["wob"], inc=bsem)

    wstg = nc.alloc_sbuf_tensor(tag + "wstg", [128, 4096], F32).ap()
    wbf = nc.alloc_sbuf_tensor(tag + "wbf", [128, 2 * 4096], BF16).ap()
    wring = Ring(b, "wr", 2)
    qsb = nc.alloc_sbuf_tensor(tag + "qsb", [128, 1024], BF16).ap()
    vtm = nc.alloc_sbuf_tensor(tag + "vtm", [128, 1024], BF16).ap()
    Qh = [nc.alloc_sbuf_tensor(tag + f"Q{h}", [65, 1024], BF16).ap() for h in range(H)]
    vstat = nc.alloc_sbuf_tensor(tag + "vst", [128, 8 * 768], BF16).ap()

    wcast = {}
    evms = {}
    for oc in range(12):
        par = oc % 2
        wb = wbf[:, par * 4096:(par + 1) * 4096]
        ldeps = []
        if oc - 1 in wcast:
            ldeps.append(wcast[oc - 1])
        lf = wring.dma(oc, "q",
                       wstg.rearrange("p (i o) -> p i o", i=FC),
                       inp["wqkv"][oc].rearrange("i p o -> p i o"),
                       deps=ldeps)
        cdeps = [lf]
        if oc >= 2:
            cdeps.append((tsem, evms[("mm", oc - 2)]))
        if oc % 2 == 0:
            wcast[oc] = (b.sem("ssem"),
                         b.op("s", lambda h, wb=wb: h.copy(wb, wstg), deps=cdeps))
        else:
            wcast[oc] = (b.sem("vsem"),
                         b.op("v", lambda h, wb=wb: h.tensor_copy(wb, wstg), deps=cdeps))
        ps = psA if par == 0 else psB
        mmdeps = [wcast[oc], (vsem, ln_done)]
        if oc >= 2:
            mmdeps.append((vsem, evms[("ev", oc - 2)]))
        last = None
        for ic in range(FC):
            for half in range(2):
                last = b.op("t", lambda h, ic=ic, half=half, ps=ps, wb=wb: h.matmul(
                    ps[:, half * 512:(half + 1) * 512],
                    wb[:, ic * 128:(ic + 1) * 128],
                    xbf[:, ic * T + half * 512:ic * T + (half + 1) * 512],
                    start=(ic == 0), stop=(ic == FC - 1)),
                    deps=mmdeps if ic == 0 else (),
                    inc=(tsem if (ic == FC - 1 and half == 1) else None))
        evms[("mm", oc)] = last
        qcs_ = b.sem("qc"); kcs_ = b.sem("kc")
        ev = b.op("v", lambda h, ps=ps, oc=oc: h.tensor_scalar(
            qsb, ps[:, :], bq[:, oc:oc + 1], None, op0=ALU.add),
            deps=[(tsem, last), (bsem, 32), (qcs_, qcs_.n), (kcs_, kcs_.n)])
        if oc < 4:
            for p2 in range(2):
                h_ = 2 * oc + p2
                evms[(h_, "q")] = b.dma("g", Qh[h_][0:64, :], qsb[p2 * 64:(p2 + 1) * 64, :],
                                        deps=[(vsem, ev)], inc=b.sem("qc"))
            evms[("ev", oc)] = ev
        elif oc < 8:
            for p2 in range(2):
                h_ = 2 * (oc - 4) + p2
                evms[(h_, "k")] = b.dma("g", inp["kvc"][h_ * 64:(h_ + 1) * 64, :],
                                        qsb[p2 * 64:(p2 + 1) * 64, :],
                                        deps=[(vsem, ev)], inc=b.sem("kc"))
            evms[("ev", oc)] = ev
        else:
            vdone = None
            for tt in range(NT):
                tms = b.op("t", lambda h, tt=tt: h.transpose(
                    psT, qsb[:, tt * 128:(tt + 1) * 128], consts["identB"]),
                    deps=[(vsem, ev)] + ([(vsem, vdone)] if vdone else []), inc=tsem)
                for p2 in range(2):
                    h_ = 2 * (oc - 8) + p2
                    vdone = b.op("v", lambda h, tt=tt, p2=p2, h_=h_: h.tensor_copy(
                        vstat[:, h_ * 768 + tt * 96:h_ * 768 + tt * 96 + 64],
                        psT[:, p2 * 64:(p2 + 1) * 64]),
                        deps=[(tsem, tms)])
            evms[("ev", oc)] = vdone
    on = b.op("v", lambda h: h.memset(
        vstat.rearrange("p (h k c) -> p h k c", h=8, k=8)[:, :, :, 64:65], 1.0),
        deps=[(vsem, evms[("ev", 11)])])
    vc = b.dma("g", inp["kvc"][512:1280, :].rearrange("r c -> (r c)").rearrange(
        "(h p c) -> p h c", h=8, p=128),
        vstat.rearrange("p (h c) -> p h c", h=8),
        deps=[(vsem, on)], inc=b.sem("kc"))
    b.wait("g", b.sem("kc"), b.sem("kc").n)
    b.wait("g", b.sem("qc"), b.sem("qc").n)
    cc = b.op("g", lambda h: h.collective_compute(
        "AllGather", ALU.bypass, replica_groups=[list(range(8))],
        ins=[inp["kvc"]], outs=[inp["kva"]]))

    proj_done = evms[("mm", 11)]
    kp = nc.alloc_sbuf_tensor_at(tag + "kp", [65, 8192], BF16, offset=arena_off).ap()
    statb = nc.alloc_sbuf_tensor_at(tag + "stat", [128, 6144], BF16,
                                    offset=arena_off + 16384).ap()
    PT = nc.alloc_sbuf_tensor_at(tag + "PT", [128, 4 * 1024], BF16,
                                 offset=arena_off + 16384 + 12288).ap()
    Bz = nc.alloc_sbuf_tensor_at(tag + "Bz", [128, 1024], F32,
                                 offset=arena_off + 36864).ap()
    rzb65 = nc.alloc_sbuf_tensor_at(tag + "rzb", [65, 1024], BF16,
                                    offset=arena_off + 40960).ap()
    on64 = nc.alloc_sbuf_tensor_at(tag + "on64", [65, 128], BF16,
                                   offset=arena_off + 43008).ap()
    subc = nc.alloc_sbuf_tensor_at(tag + "subc", [128, 16], F32,
                                   offset=arena_off + 43264).ap()
    subcb = nc.alloc_sbuf_tensor_at(tag + "subcb", [128, 16], BF16,
                                    offset=arena_off + 43328).ap()
    ao = [nc.alloc_sbuf_tensor(tag + f"ao{h}", [64, 1024], BF16).ap() for h in range(H)]

    o64 = b.sem("o64")
    b.dma("q", on64[64:65, :], inp["c_ones1"], deps=[(tsem, proj_done)], inc=o64)
    km = b.op("v", lambda h: h.memset(kp[64:65, :], 1.0), deps=[(tsem, proj_done)])
    kr = Ring(b, "kr", 1)
    sr_ = Ring(b, "sr", 1)
    mr = Ring(b, "mr", 2)
    kva = inp["kva"]
    aoms = {}
    prev_ep = None
    prev_exp = {}
    prev_mback = {}
    for h_ in range(H):
        kdeps = [(gsem, cc), (vsem, km)]
        if h_ > 0:
            kdeps.append((tsem, aoms[("send", h_ - 1)]))
        ldk = kr.dma(0, "q",
                     kp[0:64, :].rearrange("p (r t) -> p r t", r=8),
                     kva.rearrange("(r q) t -> r q t", r=8)[:, h_ * 64:(h_ + 1) * 64, :]
                        .rearrange("r p t -> p r t"),
                     deps=kdeps)
        sdeps = [(gsem, cc)]
        if h_ > 0:
            sdeps.append((tsem, aoms[("avend", h_ - 1)]))
        OFF = (512 + h_ * 96) * 1024
        lds = sr_.dma(0, "q",
                      statb.rearrange("p (r c) -> p r c", r=8),
                      kva.rearrange("r c -> (r c)").rearrange("(r a) -> r a", r=8)
                         [:, OFF:OFF + 98304].rearrange("r (p c) -> p r c", p=128),
                      deps=sdeps)
        qcs = b.sem("qc")
        for qt in range(NT):
            ps = psB[:, (qt % 2) * 512:(qt % 2) * 512 + 512]
            mdeps = [ldk, (qcs, qcs.n)]
            if qt >= 2:
                mdeps.append((vsem, aoms[("sub", h_, qt - 2)]))
            if qt < 2 and h_ > 0:
                mdeps.append((vsem, aoms[("e4", h_ - 1)]))
            sm = b.op("t", lambda h, qt=qt, ps=ps, h_=h_: h.matmul(
                ps, Qh[h_][0:64, qt * 128:(qt + 1) * 128], kp[0:64, 0:8192:16],
                start=True, stop=True), deps=mdeps, inc=tsem)
            b.op("v", lambda h, qt=qt, ps=ps: h.tensor_reduce(
                subc[:, qt:qt + 1], ps, axis=AX.X, op=ALU.max), deps=[(tsem, sm)])
            aoms[("sub", h_, qt)] = b.op("v", lambda h, qt=qt: h.tensor_scalar(
                subcb[:, qt:qt + 1], subc[:, qt:qt + 1], -1.0, None, op0=ALU.mult))
        mdma = mr.dma(h_, "g", inp["mrow"][h_ % 2:h_ % 2 + 1, :]
                      .rearrange("one (q p) -> (one p) q", p=128),
                      subcb[:, 0:8],
                      deps=[(vsem, aoms[("sub", h_, NT - 1)])] +
                           ([prev_mback[h_ % 2]] if h_ % 2 in prev_mback else []),
                      allow_slow_non_contiguous=True)
        mback = mr.dma(h_, "g", Qh[h_][64:65, :], inp["mrow"][h_ % 2:h_ % 2 + 1, :],
                       deps=[mdma])
        prev_mback[h_ % 2] = mback
        avdeps0 = [mback, ldk, lds]
        if prev_ep is not None:
            avdeps0.append((vsem, prev_ep))
        send = avend = None
        avhist = {}
        exps = {}

        def emit_av(kc):
            nonlocal avend
            for half in range(2):
                avend = b.op("t", lambda h, kc=kc, half=half, h_=h_: h.matmul(
                    psAV[0:96, half * 512:(half + 1) * 512],
                    statb[:, (kc // 8) * 768 + (kc % 8) * 96:(kc // 8) * 768 + (kc % 8) * 96 + 96],
                    PT[:, (kc % 4) * 1024 + half * 512:(kc % 4) * 1024 + (half + 1) * 512],
                    start=(kc == 0), stop=(kc == 63)),
                    deps=[(ssem, exps[kc])] if half == 0 else (),
                    inc=(tsem if half == 1 else None))
            avhist[kc] = avend

        for kc in range(64):
            pss = psS if kc % 2 == 0 else psA
            sdeps2 = list(avdeps0) if kc < 2 else []
            if kc >= 2:
                sdeps2.append((ssem, prev_exp[kc % 2]))
            for half in range(2):
                send = b.op("t", lambda h, kc=kc, half=half, pss=pss, h_=h_: h.matmul(
                    pss[:, half * 512:(half + 1) * 512],
                    kp[:, kc * 128:(kc + 1) * 128],
                    Qh[h_][:, half * 512:(half + 1) * 512],
                    start=True, stop=True),
                    deps=sdeps2 if half == 0 else (),
                    inc=(tsem if half == 1 else None))
            pexp = b.op("s", lambda h, kc=kc, pss=pss: h.activation(
                PT[:, (kc % 4) * 1024:(kc % 4 + 1) * 1024], pss[:, :], AF.Exp),
                deps=[(tsem, send)] + ([(tsem, avhist[kc - 4])] if kc >= 4 else []))
            prev_exp[kc % 2] = pexp
            exps[kc] = pexp
            if kc >= 1:
                emit_av(kc - 1)
        emit_av(63)
        aoms[("send", h_)] = send
        aoms[("avend", h_)] = avend
        e1 = b.op("v", lambda h: h.reciprocal(Bz[64:65, :], psAV[64:65, :]),
                  deps=[(tsem, avend)])
        e2 = b.op("v", lambda h: h.tensor_scalar(Bz[64:65, :], Bz[64:65, :],
                  1.0 / (512.0 ** 0.5), None, op0=ALU.mult))
        e3 = b.op("v", lambda h: h.tensor_copy(rzb65[64:65, :], Bz[64:65, :]))
        bm = None
        for half in range(2):
            bm = b.op("t", lambda h, half=half: h.matmul(
                psB[:, half * 512:(half + 1) * 512], on64[64:65, :],
                rzb65[64:65, half * 512:(half + 1) * 512], start=True, stop=True),
                deps=[(vsem, e3), (o64, 16)] + ([(vsem, aoms[("sub", h_, NT - 1)])] if True else []),
                inc=(tsem if half == 1 else None))
        e4 = b.op("v", lambda h: h.tensor_copy(Bz, psB[:, :]), deps=[(tsem, bm)])
        aoms[("e4", h_)] = e4
        import os as _o
        if _o.environ.get("DBGH0") and h_ == 0 and inp.get("dbg") is not None:
            zst = nc.alloc_sbuf_tensor(tag + "zst", [65, 1024], F32).ap()
            dsm = b.sem("dbg")
            def dump1(dst, srcfn, deps=()):
                zc_ = b.op("v", srcfn, deps=list(deps))
                m_ = b.dma("q", dst, zst[0:dst.shape[0], 0:dst.shape[1]], deps=[(vsem, zc_)], inc=dsm)
                b.wait("v", dsm, m_)
            dump1(inp["dbg"][0:1, :], lambda h: h.tensor_copy(zst[0:1, :], psAV[64:65, :].broadcast_to((1, 1024)) if False else psAV[64:65, :]))
            dump1(inp["dbg"][1:2, :], lambda h: h.tensor_copy(zst[0:1, :], Qh[0][64:65, :]))
            dump1(inp["dbg"][66:131, :], lambda h: h.tensor_copy(zst[0:65, :], kp[0:65, 0:1024]))
            dump1(inp["dbg"][131:196, :], lambda h: h.tensor_copy(zst[0:65, :], Qh[0][0:65, :]))
            dump1(inp["dbg"][196:260, 0:96], lambda h: h.tensor_copy(zst[0:64, 0:96], statb[0:64, 0:96]), deps=[lds])
            dump1(inp["dbg"][260:324, 0:96], lambda h: h.tensor_copy(zst[0:64, 0:96], vstat[0:64, 0:96]))
            kvcsl = inp["kvc"][512:1280, :].rearrange("r c -> (r c)").rearrange("(p c) -> p c", p=128)
            dump1(inp["dbg"][324:388, 0:96], lambda h: h.tensor_copy(zst[0:64, 0:96], zst[0:64, 0:96]))
        prev_ep = b.op("v", lambda h, h_=h_: h.tensor_tensor(
            ao[h_], psAV[0:64, :], Bz[0:64, :], op=ALU.mult), deps=[(vsem, e4)])
        aoms[("ep", h_)] = prev_ep

    woT = nc.alloc_sbuf_tensor_at(tag + "woT", [64, 2 * 1024], BF16, offset=arena_off).ap()
    wostg = nc.alloc_sbuf_tensor_at(tag + "wostg", [64, 2 * 1024], F32,
                                    offset=arena_off + 4096).ap()
    xs = nc.alloc_sbuf_tensor_at(tag + "xs", [128, 2 * 1024], F32,
                                 offset=arena_off + 4096 + 8192).ap()
    yo = nc.alloc_sbuf_tensor_at(tag + "yo", [128, 2 * 1024], F32,
                                 offset=arena_off + 4096 + 8192 + 8192).ap()
    wor = Ring(b, "wor", 2)
    xr = Ring(b, "xl", 2)
    so = Ring(b, "so", 2)
    ep_all = prev_ep
    ld_x = {}; ev2 = {}; st2 = {}; mm2 = {}; wc2 = {}
    for oc in range(FC):
        par = oc % 2
        wdeps = [(vsem, ep_all)]
        if oc - 2 in mm2:
            wdeps.append((tsem, mm2[oc - 2]))
        lw = wor.dma(oc, "q", wostg[0:64, par * 1024:(par + 1) * 1024]
                     .rearrange("p (i o) -> p i o", i=8),
                     inp["wo"][oc].rearrange("i p o -> p i o"), deps=wdeps)
        wc2[oc] = b.op("v", lambda h, par=par: h.tensor_copy(
            woT[:, par * 1024:(par + 1) * 1024], wostg[0:64, par * 1024:(par + 1) * 1024]),
            deps=[lw])
        ld_x[oc] = xr.dma(oc, "q", xs[:, par * 1024:(par + 1) * 1024],
                          x_in[oc * 128:(oc + 1) * 128, :],
                          deps=[(vsem, ev2[oc - 2])] if oc - 2 in ev2 else [(vsem, ep_all)])
        ps = psA if par == 0 else psS
        mmdeps = [(vsem, wc2[oc]), (vsem, ep_all)]
        if oc - 2 in ev2:
            mmdeps.append((vsem, ev2[oc - 2]))
        last = None
        for ic in range(8):
            for half in range(2):
                last = b.op("t", lambda h, ic=ic, half=half, ps=ps, par=par: h.matmul(
                    ps[:, half * 512:(half + 1) * 512],
                    woT[0:64, par * 1024 + ic * 128:par * 1024 + (ic + 1) * 128],
                    ao[ic][:, half * 512:(half + 1) * 512],
                    start=(ic == 0), stop=(ic == 7)),
                    deps=mmdeps if ic == 0 else (),
                    inc=(tsem if (ic == 7 and half == 1) else None))
        mm2[oc] = last
        ev2[oc] = b.op("v", lambda h, oc=oc, ps=ps, par=par: h.scalar_tensor_tensor(
            yo[:, par * 1024:(par + 1) * 1024], ps[:, :], 1.0,
            xs[:, par * 1024:(par + 1) * 1024], op0=ALU.bypass, op1=ALU.add),
            deps=[(tsem, last), ld_x[oc], (bsem, 32)] + ([st2[oc - 2]] if oc - 2 in st2 else []))
        ev2[oc] = b.op("v", lambda h, oc=oc, par=par: h.tensor_scalar(
            yo[:, par * 1024:(par + 1) * 1024], yo[:, par * 1024:(par + 1) * 1024],
            bo[:, oc:oc + 1], None, op0=ALU.add))
        st2[oc] = so.dma(oc, "q", x_out[oc * 128:(oc + 1) * 128, :],
                         yo[:, par * 1024:(par + 1) * 1024], deps=[(vsem, ev2[oc])])
    import os as _o2
    if _o2.environ.get("DBGH0") and inp.get("dbg") is not None:
        dst = nc.alloc_sbuf_tensor(tag + "dbao", [64, 1024], F32).ap()
        dc = b.op("v", lambda h: h.tensor_copy(dst, ao[0]), deps=[(vsem, ep_all)])
        b.dma("q", inp["dbg"][2:66, :], dst, deps=[(vsem, dc)], inc=b.sem("dbg"))
        b.wait("q", b.sem("dbg"), b.sem("dbg").n)
    for s in so.sems:
        b.wait("q", s, s.n)
    return st2


def host_l1_attn_inputs(Wq, Wk, Wv, Wo, Wg, Win, dt_bias, A_log, gnorm, core_id):
    def tl(w):
        return tile_lhsT(np.ascontiguousarray(w[:, ORDER].T)).astype(np.float32)
    wq = tl(Wq); wk = tl(Wk); wv = tl(Wv); wg = tl(Wg)
    win = np.ascontiguousarray(Win[:, ORDER].T.reshape(FC, 128, 32)).astype(np.float32)
    WoP = Wo[ORDER, :]
    wo = tile_lhsT(np.ascontiguousarray(WoP.T)).astype(np.float32)
    Dh = 128
    inv = 1.0 / (10000.0 ** (np.arange(0, Dh, 2) / Dh))
    pos = core_id * T + np.arange(T)
    ang = pos[:, None] * inv[None, :]
    cos = np.concatenate([np.cos(ang), np.cos(ang)], -1)
    sin = np.concatenate([np.sin(ang), np.sin(ang)], -1)
    cs = cos.reshape(NT, 128, Dh).transpose(1, 0, 2).reshape(128, NT * Dh)
    sn = sin.reshape(NT, 128, Dh).transpose(1, 0, 2).reshape(128, NT * Dh)
    l_ = np.arange(128)
    maskGT = (l_[:, None] > l_[None, :]).astype(BF)
    LTc = (l_[:, None] <= l_[None, :]).astype(BF)
    UTc = (l_[:, None] <= l_[None, :]).astype(BF)
    M = np.zeros((9, 8), np.float32)
    c = core_id
    for bb in range(8):
        for j in range(8):
            if bb < j < c:
                M[j, bb] = 1.0
        if not (bb < c):
            M[8, bb] = 1.0
    return dict(wq=wq, wk=wk, wv=wv, wg=wg, win=win, wo=wo,
                dtb=np.ascontiguousarray(dt_bias.reshape(32, 1)).astype(np.float32),
                alog=np.ascontiguousarray(A_log.reshape(32, 1)).astype(np.float32),
                gnorm=np.ascontiguousarray(gnorm.reshape(128, 1)).astype(np.float32),
                ropec=cs.astype(BF), ropes=sn.astype(BF),
                maskgt=maskGT, ltc=LTc, utc=UTc, mext=M.astype(BF))


def build_l1_attn(b, x_in, x_out, inp, tag):
    nc = b.nc
    tsem = b.sem("tsem"); vsem = b.sem("vsem"); ssem = b.sem("ssem"); gsem = b.sem("gsem")
    consts = inp["consts"]
    H, KVH, NCH = 32, 8, NT

    psA = nc.alloc_psum_tensor(tag + "psA", [128, 1024], F32).ap()
    psB = nc.alloc_psum_tensor(tag + "psB", [128, 1024], F32).ap()
    psC = nc.alloc_psum_tensor(tag + "psC", [128, 1024], F32).ap()
    psD = nc.alloc_psum_tensor(tag + "psD", [128, 1024], F32).ap()
    psT2 = psD.tensor.bitcast(BF16).ap()[:, 0:128]

    scr_off = nc.sbuf_base
    nc.bump_sbuf(45 * 1024)
    xbf = alloc_fm(nc, tag + "xbf", BF16)
    ln_done = ln_fm(b, tag + "ln", x_in, xbf, consts, psA, psB, scratch_off=scr_off)

    sml = nc.alloc_sbuf_tensor(tag + "sml", [128, 8], F32).ap()
    dtb = sml[0:32, 0:1]; alog = sml[0:32, 1:2]; gnormc = sml[:, 2:3]
    ssm = b.sem("sm")
    b.dma("q", dtb, inp["dtb"], inc=ssm)
    b.dma("q", alog, inp["alog"], inc=ssm)
    b.dma("q", gnormc, inp["gnorm"], inc=ssm)
    rope_c = nc.alloc_sbuf_tensor(tag + "ropec", [128, 1024], BF16).ap()
    rope_s = nc.alloc_sbuf_tensor(tag + "ropes", [128, 1024], BF16).ap()
    b.dma("q", rope_c, inp["ropec"], inc=ssm)
    b.dma("q", rope_s, inp["ropes"], inc=ssm)
    mgt = nc.alloc_sbuf_tensor(tag + "mgt", [128, 128], BF16).ap()
    ltc = nc.alloc_sbuf_tensor(tag + "ltc", [128, 128], BF16).ap()
    utc = nc.alloc_sbuf_tensor(tag + "utc", [128, 128], BF16).ap()
    mext = nc.alloc_sbuf_tensor(tag + "mext", [9, 8], BF16).ap()
    b.dma("q", mgt, inp["maskgt"], inc=ssm)
    b.dma("q", ltc, inp["ltc"], inc=ssm)
    b.dma("q", utc, inp["utc"], inc=ssm)
    b.dma("q", mext, inp["mext"], inc=ssm)
    SM_ALL = ssm.n

    win_sb = nc.alloc_sbuf_tensor(tag + "win", [128, FC * 32], BF16).ap()
    wsg = nc.alloc_sbuf_tensor_at(tag + "wsg", [128, 2 * 2048], F32, offset=scr_off).ap()
    wbfA = nc.alloc_sbuf_tensor_at(tag + "wbfA", [128, 4096], BF16, offset=scr_off + 16384).ap()
    wbfB = nc.alloc_sbuf_tensor_at(tag + "wbfB", [128, 4096], BF16, offset=scr_off + 24576).ap()
    wring = Ring(b, "wr", 2)
    wiring = Ring(b, "wir", 1)
    wcast = None
    lw = wiring.dma(0, "q", wsg[:, 0:1024].rearrange("p (i m) -> p i m", i=FC),
                    inp["win"].rearrange("i p m -> p i m"), deps=[(vsem, ln_done)])
    wcast = b.op("v", lambda h: h.tensor_copy(win_sb, wsg[:, 0:1024]), deps=[lw])
    t_dt = None
    for ic in range(FC):
        for half in range(2):
            t_dt = b.op("t", lambda h, ic=ic, half=half: h.matmul(
                psA[0:32, half * 512:(half + 1) * 512],
                win_sb[:, ic * 32:(ic + 1) * 32],
                xbf[:, ic * T + half * 512:ic * T + (half + 1) * 512],
                start=(ic == 0), stop=(ic == FC - 1)),
                deps=[(vsem, wcast), (vsem, ln_done)] if ic == 0 else (),
                inc=(tsem if (ic == FC - 1 and half == 1) else None))
    zf = nc.alloc_sbuf_tensor_at(tag + "zf", [32, 1024], F32, offset=scr_off).ap()
    dtv = nc.alloc_sbuf_tensor_at(tag + "dtv", [32, 1024], F32, offset=scr_off + 4096).ap()
    lndt = nc.alloc_sbuf_tensor_at(tag + "lndt", [32, 1024], F32, offset=scr_off + 8192).ap()
    gf = nc.alloc_sbuf_tensor_at(tag + "gf", [32, 1024], F32, offset=scr_off + 12288).ap()
    Gf = nc.alloc_sbuf_tensor_at(tag + "Gf", [32, 1024], F32, offset=scr_off + 16384).ap()
    kdf = nc.alloc_sbuf_tensor_at(tag + "kdf", [32, 1024], F32, offset=scr_off + 20480).ap()
    ghi = nc.alloc_sbuf_tensor_at(tag + "ghi", [32, 1024], BF16, offset=scr_off + 24576).ap()
    glo = nc.alloc_sbuf_tensor_at(tag + "glo", [32, 1024], BF16, offset=scr_off + 26624).ap()
    m1 = b.op("v", lambda h: h.tensor_scalar(zf, psA[0:32, :], dtb, None, op0=ALU.add),
              deps=[(tsem, t_dt), (ssm, SM_ALL)])
    m2 = b.op("s", lambda h: h.activation(dtv, zf, AF.Abs), deps=[(vsem, m1)])
    m3 = b.op("s", lambda h: h.activation(dtv, dtv, AF.Exp, scale=-1.0))
    m4 = b.op("v", lambda h: h.tensor_scalar(dtv, dtv, 1.0, None, op0=ALU.add),
              deps=[(ssem, m3)])
    m5 = b.op("s", lambda h: h.activation(dtv, dtv, AF.Ln), deps=[(vsem, m4)])
    m6 = b.op("s", lambda h: h.activation(lndt, zf, AF.Relu))
    m7 = b.op("v", lambda h: h.tensor_tensor(dtv, dtv, lndt, op=ALU.add),
              deps=[(ssem, m6)])
    m8 = b.op("s", lambda h: h.activation(lndt, dtv, AF.Ln), deps=[(vsem, m7)])
    m9 = b.op("s", lambda h: h.activation(sml[0:32, 3:4], alog, AF.Exp, scale=1.0))
    m10 = b.op("v", lambda h: h.tensor_scalar(sml[0:32, 3:4], sml[0:32, 3:4], -1.0, None, op0=ALU.mult),
               deps=[(ssem, m9)])
    m11 = b.op("v", lambda h: h.tensor_scalar(gf, dtv, sml[0:32, 3:4], None, op0=ALU.mult))
    gdone = None
    for n in range(NCH):
        gdone = b.op("v", lambda h, n=n: h.tensor_tensor_scan(
            Gf[:, n * 128:(n + 1) * 128], gf[:, n * 128:(n + 1) * 128],
            gf[:, n * 128:(n + 1) * 128], 0.0, op0=ALU.add, op1=ALU.bypass))
    b.op("v", lambda h: h.tensor_copy(ghi, gf))
    b.op("v", lambda h: h.tensor_tensor(kdf, gf, ghi, op=ALU.subtract))
    ghdone = b.op("v", lambda h: h.tensor_copy(glo, kdf))
    for n in range(NCH):
        b.op("v", lambda h, n=n: h.tensor_scalar(
            kdf[:, n * 128:(n + 1) * 128], Gf[:, n * 128:(n + 1) * 128],
            Gf[:, n * 128 + 127:n * 128 + 128], None, op0=ALU.subtract))
    mkd = b.op("v", lambda h: h.tensor_tensor(kdf, kdf, lndt, op=ALU.subtract))
    kddone = b.op("s", lambda h: h.activation(kdf, kdf, AF.Exp, scale=-1.0),
                  deps=[(vsem, mkd)])
    lamf = nc.alloc_sbuf_tensor(tag + "lamf", [32, 8], F32).ap()
    lamdone = None
    for n in range(NCH):
        lamdone = b.op("s", lambda h, n=n: h.activation(
            lamf[:, n:n + 1], Gf[:, n * 128 + 127:n * 128 + 128], AF.Exp),
            deps=[(vsem, gdone)])
    lnlo = sml[0:32, 4:5]
    b.op("v", lambda h: h.tensor_copy(lnlo, Gf[:, 127:128]))
    for n in range(1, NCH):
        b.op("v", lambda h, n=n: h.tensor_tensor(
            lnlo, lnlo, Gf[:, n * 128 + 127:n * 128 + 128], op=ALU.add))
    lnldone = b.op("v", lambda h: h.tensor_copy(lnlo, lnlo))

    tmt = nc.alloc_sbuf_tensor(tag + "tmt", [128, NCH * 4 * 32], F32).ap()
    tmb = nc.alloc_sbuf_tensor(tag + "tmb", [128, NCH * 2 * 32], F32).ap()
    lamrow = nc.alloc_sbuf_tensor(tag + "lamrow", [1, 32], BF16).ap()
    def G_tm(n): return tmt[:, (n * 4 + 0) * 32:(n * 4 + 1) * 32]
    def lndt_tm(n): return tmt[:, (n * 4 + 1) * 32:(n * 4 + 2) * 32]
    def expG_tm(n): return tmt[:, (n * 4 + 2) * 32:(n * 4 + 3) * 32]
    def kdsc_tm(n): return tmt[:, (n * 4 + 3) * 32:(n * 4 + 4) * 32]
    def ghi_tm(n): return tmb[:, (n * 2 + 0) * 32:(n * 2 + 1) * 32]
    def glo_tm(n): return tmb[:, (n * 2 + 1) * 32:(n * 2 + 2) * 32]
    lamb = nc.alloc_sbuf_tensor(tag + "lamb", [128, NCH * 32], F32).ap()
    tprev = None
    for n in range(NCH):
        for (src, dstf, idn) in ((Gf, G_tm, "identF"), (lndt, lndt_tm, "identF"),
                                 (kdf, kdsc_tm, "identF")):
            idsl = consts[idn][0:32, 0:32]
            tms = b.op("t", lambda h, src=src, n=n, idsl=idsl: h.transpose(
                psD[:, 256:288], src[:, n * 128:(n + 1) * 128], idsl),
                deps=[(vsem, ghdone), (ssem, kddone), (vsem, tprev)] if tprev else
                     [(vsem, ghdone), (ssem, kddone)], inc=tsem)
            tprev = b.op("v", lambda h, dstf=dstf, n=n: h.tensor_copy(dstf(n), psD[:, 256:288]),
                         deps=[(tsem, tms)])
        for (src, dstf, idn) in ((ghi, ghi_tm, "identB"), (glo, glo_tm, "identB")):
            idsl = consts[idn][0:32, 0:32]
            tms = b.op("t", lambda h, src=src, n=n, idsl=idsl: h.transpose(
                psT2[:, 0:32], src[:, n * 128:(n + 1) * 128], idsl),
                deps=[(vsem, tprev)], inc=tsem)
            tprev = b.op("v", lambda h, dstf=dstf, n=n: h.tensor_copy(dstf(n), psT2[:, 0:32]),
                         deps=[(tsem, tms)])
        b.op("s", lambda h, n=n: h.activation(expG_tm(n), G_tm(n), AF.Exp),
             deps=[(vsem, tprev)])
        tl1 = b.op("t", lambda h, n=n: h.transpose(
            psD[0:1, 288:320], lamf[:, n:n + 1], consts["identF"][0:32, 0:32]),
            deps=[(ssem, lamdone), (vsem, tprev)], inc=tsem)
        c1 = b.op("v", lambda h: h.tensor_copy(lamrow, psD[0:1, 288:320]),
                  deps=[(tsem, tl1)])
        bl = b.op("t", lambda h, n=n: h.matmul(
            psD[:, 320:352], consts["ones1"], lamrow,
            start=True, stop=True), deps=[(vsem, c1)], inc=tsem)
        b.op("v", lambda h, n=n: h.tensor_copy(lamb[:, n * 32:(n + 1) * 32], psD[:, 320:352]),
             deps=[(tsem, bl)])
    p0b_done = b.op("v", lambda h: h.tensor_copy(lamf, lamf))
    import os as _os4
    DBG4 = _os4.environ.get("DBG4") and inp.get("dbg") is not None
    if DBG4:
        dsm4 = b.sem("d4")
        dstg = nc.alloc_sbuf_tensor(tag + "dstg", [128, 1024], F32).ap()
        def dump4(dst, srcap, deps=()):
            p, f = dst.shape
            zc_ = b.op("v", lambda h, srcap=srcap, p=p, f=f: h.tensor_copy(
                dstg[0:p, 0:f], srcap), deps=list(deps))
            m_ = b.dma("q", dst, dstg[0:p, 0:f], deps=[(vsem, zc_)], inc=dsm4)
            b.wait("v", dsm4, m_)
        dump4(inp["dbg"][0:32, :], zf)
        dump4(inp["dbg"][32:64, :], dtv)
        dump4(inp["dbg"][64:96, :], gf)
        dump4(inp["dbg"][96:128, :], Gf)
        dump4(inp["dbg"][128:160, :], kdf)
        dump4(inp["dbg"][160:192, 0:8], lamf)
        dump4(inp["dbg"][192:320, 0:32], tmt[:, 0:32])
        dump4(inp["dbg"][320:448, 0:32], lamb[:, 0:32])
        p0b_done = b.op("v", lambda h: h.tensor_copy(lamf, lamf))

    ktm_off = nc.sbuf_base
    ktm = nc.alloc_sbuf_tensor(tag + "ktm", [128, KVH * 1024], BF16).ap()
    kfm = nc.alloc_sbuf_tensor(tag + "kfm", [128, KVH * 1024], BF16).ap()
    vtm = nc.alloc_sbuf_tensor(tag + "vtm", [128, KVH * 1024], BF16).ap()
    ptmp = nc.alloc_sbuf_tensor(tag + "ptmp", [128, 1024], BF16).ap()
    states = nc.alloc_sbuf_tensor(tag + "states", [128, H * 128], F32).ap()
    kdtmp = nc.alloc_sbuf_tensor(tag + "kdtmp", [128, 1024], BF16).ap()
    cstg = nc.alloc_sbuf_tensor(tag + "cstg", [128, 1024], F32).ap()
    psT2a = psD.tensor.bitcast(BF16).ap()[:, 0:1024]
    cstd = inp["cstd"]

    _swlast = {"cast": None}

    def stream_w(widx, wsrc, deps0, eng=None):
        wb = wbfA if widx % 2 == 0 else wbfB
        d0 = list(deps0)
        if _swlast["cast"] is not None:
            d0.append(_swlast["cast"])
        lf = wring.dma(widx, "q",
                       wsg.rearrange("p (i o) -> p i o", i=FC),
                       wsrc.rearrange("i p o -> p i o"),
                       deps=d0)
        if eng is None:
            eng = "s" if widx % 2 == 0 else "v"
        if eng == "s":
            ms = b.op("s", lambda h, wb=wb: h.copy(wb, wsg), deps=[lf])
            cast = (b.sem("ssem"), ms)
        else:
            ms = b.op("v", lambda h, wb=wb: h.tensor_copy(wb, wsg), deps=[lf])
            cast = (b.sem("vsem"), ms)
        _swlast["cast"] = cast
        return wb, cast

    def proj(widx, wsrc, deps0, out_bf, psum, mm_deps=()):
        wb, cast = stream_w(widx, wsrc, deps0)
        last = None
        for ic in range(FC):
            for half in range(2):
                last = b.op("t", lambda h, ic=ic, half=half, psum=psum, wb=wb: h.matmul(
                    psum[:, half * 512:(half + 1) * 512],
                    wb[:, ic * 128:(ic + 1) * 128],
                    xbf[:, ic * T + half * 512:ic * T + (half + 1) * 512],
                    start=(ic == 0), stop=(ic == FC - 1)),
                    deps=[cast, (vsem, ln_done)] + list(mm_deps) if ic == 0 else (),
                    inc=(tsem if (ic == FC - 1 and half == 1) else None))
        ev = b.op("v", lambda h, out_bf=out_bf, psum=psum: h.tensor_copy(out_bf, psum[:, :]),
                  deps=[(tsem, last)])
        return last, ev

    def rope_tm(dst1024, dep):
        rot = ptmp
        ms = b.op("v", lambda h: h.tensor_scalar(
            rtmp.rearrange("p (n d) -> p n d", n=NCH)[:, :, 0:64],
            dst1024.rearrange("p (n d) -> p n d", n=NCH)[:, :, 64:128], -1.0, None, op0=ALU.mult),
            deps=[dep] if dep else [])
        b.op("v", lambda h: h.tensor_copy(
            rtmp.rearrange("p (n d) -> p n d", n=NCH)[:, :, 64:128],
            dst1024.rearrange("p (n d) -> p n d", n=NCH)[:, :, 0:64]))
        b.op("v", lambda h: h.tensor_tensor(dst1024, dst1024, rope_c, op=ALU.mult))
        b.op("v", lambda h: h.tensor_tensor(rtmp, rtmp, rope_s, op=ALU.mult))
        return b.op("v", lambda h: h.tensor_tensor(dst1024, dst1024, rtmp, op=ALU.add))

    rtmp = nc.alloc_sbuf_tensor(tag + "rtmp", [128, 1024], BF16).ap()

    widx = 1
    prev_tr = None
    wslot_guard = {0: None, 1: None}

    def batch_tr(src1024, dst1024, dep0):
        nonlocal_last = None
        d0 = list(dep0)
        if prev_tr is not None:
            d0.append((vsem, prev_tr))
        for n in range(NCH):
            nonlocal_last = b.op("t", lambda h, n=n: h.transpose(
                psT2a[:, n * 128:(n + 1) * 128], src1024[:, n * 128:(n + 1) * 128],
                consts["identB"]),
                deps=d0 if n == 0 else (), inc=(tsem if n == NCH - 1 else None))
        return b.op("v", lambda h: h.tensor_copy(dst1024, psT2a),
                    deps=[(tsem, nonlocal_last)])

    for kvh in range(KVH):
        d0k = [(vsem, prev_tr)] if prev_tr else [(vsem, p0b_done)]
        if wslot_guard[widx % 2] is not None:
            d0k.append((tsem, wslot_guard[widx % 2]))
        mm_k, ev_k = proj(widx, inp["wk"][kvh], d0k, ptmp, psC)
        wslot_guard[widx % 2] = mm_k; widx += 1
        ksl1024 = ktm[:, kvh * 1024:(kvh + 1) * 1024]
        prev_tr = batch_tr(ptmp, ksl1024, [(vsem, ev_k)])
        rp = rope_tm(ksl1024, (vsem, prev_tr))
        prev_tr = batch_tr(ksl1024, kfm[:, kvh * 1024:(kvh + 1) * 1024], [(vsem, rp)])
        d0v = [(vsem, prev_tr)]
        if wslot_guard[widx % 2] is not None:
            d0v.append((tsem, wslot_guard[widx % 2]))
        mm_v, ev_v = proj(widx, inp["wv"][kvh], d0v, ptmp, psC)
        wslot_guard[widx % 2] = mm_v; widx += 1
        prev_tr = batch_tr(ptmp, vtm[:, kvh * 1024:(kvh + 1) * 1024], [(vsem, ev_v)])

    zm = b.op("v", lambda h: h.memset(states, 0.0), deps=[(vsem, prev_tr)])
    csr = Ring(b, "cs", 2)
    stup = {}
    csdma = {}
    cs_prev = None
    for h_ in range(H):
        kvh = h_ // 4
        kd = None
        for n in range(NCH):
            kdeps = [(vsem, zm)]
            if cs_prev is not None:
                kdeps.append((tsem, cs_prev))
            kd = b.op("v", lambda h, kvh=kvh, n=n, h_=h_: h.tensor_scalar(
                kdtmp[:, n * 128:(n + 1) * 128],
                ktm[:, kvh * 1024 + n * 128:kvh * 1024 + (n + 1) * 128],
                kdsc_tm(n)[:, h_:h_ + 1], None, op0=ALU.mult),
                deps=kdeps if n == 0 else ())
        cs = None
        for n in range(NCH):
            cs = b.op("t", lambda h, kvh=kvh, n=n: h.matmul(
                psC[:, n * 128:(n + 1) * 128], kdtmp[:, n * 128:(n + 1) * 128],
                vtm[:, kvh * 1024 + n * 128:kvh * 1024 + (n + 1) * 128],
                start=True, stop=True), deps=[(vsem, kd)] if n == 0 else (),
                inc=(tsem if n == NCH - 1 else None))
        cs_prev = cs
        cpdeps = [(tsem, cs)]
        if (h_ - 1) in csdma:
            cpdeps.append(csdma[h_ - 1])
        cp = b.op("v", lambda h: h.tensor_copy(cstg, psC), deps=cpdeps)
        st = None
        for n in range(NCH):
            st = b.op("v", lambda h, n=n, h_=h_: h.scalar_tensor_tensor(
                states[:, h_ * 128:(h_ + 1) * 128], states[:, h_ * 128:(h_ + 1) * 128],
                lamb[:, n * 32 + h_:n * 32 + h_ + 1], cstg[:, n * 128:(n + 1) * 128],
                op0=ALU.mult, op1=ALU.add))
        stup[(h_, NCH - 1)] = st
        csdma[h_] = csr.dma(h_, "g", cstd[h_].rearrange("n p e -> p n e"),
                            cstg.rearrange("p (n e) -> p n e", n=NCH),
                            deps=[(vsem, st)])
    if DBG4:
        dump4(inp["dbg"][448:576, :], states[:, 0:1024], deps=[(vsem, stup[(7, NCH - 1)])])
    ctsem = b.sem("ct")
    c1m = b.dma("q", inp["stc"][0:4096, :].rearrange("(hh p) e -> p hh e", p=128),
                states.rearrange("p (hh e) -> p hh e", hh=H),
                deps=[(vsem, stup[(H - 1, NCH - 1)])], inc=ctsem)
    c2m = b.dma("q", inp["stc"][4096:4128, 0:1], lnlo, deps=[(vsem, lnldone)], inc=ctsem,
                allow_slow_non_contiguous=True)
    b.wait("g", ctsem, ctsem.n)
    for s in csr.sems:
        b.wait("g", s, s.n)
    cc = b.op("g", lambda h: h.collective_compute(
        "AllGather", ALU.bypass, replica_groups=[list(range(8))],
        ins=[inp["stc"]], outs=[inp["sta"]]))

    qt = nc.alloc_sbuf_tensor_at(tag + "qt", [128, 1024], BF16, offset=ktm_off).ap()
    qf = nc.alloc_sbuf_tensor_at(tag + "qf", [128, 1024], BF16, offset=ktm_off + 2048).ap()
    yt = nc.alloc_sbuf_tensor_at(tag + "yt", [128, 1024], F32, offset=ktm_off + 4096).ap()
    dsb = nc.alloc_sbuf_tensor_at(tag + "dsb", [128, 1024], F32, offset=ktm_off + 8192).ap()
    gsil = nc.alloc_sbuf_tensor_at(tag + "gsil", [128, 1024], BF16, offset=ktm_off + 12288).ap()
    ytg = nc.alloc_sbuf_tensor_at(tag + "ytg", [128, 1024], BF16, offset=ktm_off + 14336).ap()
    gr2hi = nc.alloc_sbuf_tensor(tag + "gr2hi", [128, 1024], BF16).ap()
    gr2lo = nc.alloc_sbuf_tensor(tag + "gr2lo", [128, 1024], BF16).ap()
    decb = gr2lo
    PT2s = gr2hi
    prevball = kdtmp
    rstd8 = nc.alloc_sbuf_tensor(tag + "rstd8", [128, 8], F32).ap()
    cstl = cstg
    clr = Ring(b, "cl", 2)
    ysr = Ring(b, "ys", 2)
    ydram = inp["ydram"]
    utc_b = utc.rearrange("p (o c) -> p o c", o=1).broadcast_to((128, NCH, 128))
    decb3 = decb.rearrange("p (n c) -> p n c", c=128)
    dsb3 = dsb.rearrange("p (n c) -> p n c", c=128)
    psA2a = psA.tensor.bitcast(BF16).ap()[:, 0:1024]

    pt_prev = None
    sg_prev = None
    y2_last = None
    ym_prev = None
    chn = None
    ystore = {}
    frnt = {}
    fstash = {}

    def front_a(h_):
        nonlocal widx
        d0 = []
        if wslot_guard[widx % 2] is not None:
            d0.append((tsem, wslot_guard[widx % 2]))
        wb, cast = stream_w(widx, inp["wq"][h_], d0, eng="v")
        mmd = [cast, (vsem, ln_done)]
        if pt_prev is not None:
            mmd.append((vsem, pt_prev))
        last = None
        for ic in range(FC):
            for half in range(2):
                last = b.op("t", lambda h, ic=ic, half=half, wb=wb: h.matmul(
                    psA[:, half * 512:(half + 1) * 512],
                    wb[:, ic * 128:(ic + 1) * 128],
                    xbf[:, ic * T + half * 512:ic * T + (half + 1) * 512],
                    start=(ic == 0), stop=(ic == FC - 1)),
                    deps=mmd if ic == 0 else (),
                    inc=(tsem if (ic == FC - 1 and half == 1) else None))
        wslot_guard[widx % 2] = last; widx += 1
        fstash[h_] = last

    def front_b(h_):
        ev_q = b.op("v", lambda h: h.tensor_copy(ptmp, psA[:, :]),
                    deps=[(tsem, fstash[h_])])
        tq = None
        for n in range(NCH):
            tq = b.op("t", lambda h, n=n: h.transpose(
                psA2a[:, n * 128:(n + 1) * 128], ptmp[:, n * 128:(n + 1) * 128],
                consts["identB"]),
                deps=[(vsem, ev_q)] if n == 0 else (),
                inc=(tsem if n == NCH - 1 else None))
        qtc = b.op("v", lambda h: h.tensor_copy(qt, psA2a), deps=[(tsem, tq)])
        rp = rope_tm(qt, None)
        tq2 = None
        for n in range(NCH):
            tq2 = b.op("t", lambda h, n=n: h.transpose(
                psA2a[:, n * 128:(n + 1) * 128], qt[:, n * 128:(n + 1) * 128],
                consts["identB"]),
                deps=[(vsem, rp)] if n == 0 else (),
                inc=(tsem if n == NCH - 1 else None))
        frnt[h_] = b.op("v", lambda h: h.tensor_copy(qf, psA2a), deps=[(tsem, tq2)])

    def pre_chain(h_, cl):
        nonlocal chn
        chdeps = [cl]
        if y2_last is not None:
            chdeps.append((tsem, y2_last))
        for n in range(NCH):
            b.op("v", lambda h, n=n, h_=h_: h.tensor_copy(
                prevball[:, n * 128:(n + 1) * 128], states[:, h_ * 128:(h_ + 1) * 128]),
                deps=chdeps if n == 0 else ())
            chn = b.op("v", lambda h, n=n, h_=h_: h.scalar_tensor_tensor(
                states[:, h_ * 128:(h_ + 1) * 128], states[:, h_ * 128:(h_ + 1) * 128],
                lamb[:, n * 32 + h_:n * 32 + h_ + 1], cstl[:, n * 128:(n + 1) * 128],
                op0=ALU.mult, op1=ALU.add))

    front_a(0)
    front_b(0)
    cl_h = {0: clr.dma(0, "q", cstl.rearrange("p (n e) -> p n e", n=NCH),
                       cstd[0].rearrange("n p e -> p n e"), deps=[(gsem, cc)])}
    lnl9 = nc.alloc_sbuf_tensor(tag + "lnl9", [9, 64], F32).ap()
    lnl9b = nc.alloc_sbuf_tensor(tag + "lnl9b", [9, 32], BF16).ap()
    coefbc = nc.alloc_sbuf_tensor(tag + "coefbc", [128, 256], F32).ap()
    cro = nc.alloc_sbuf_tensor(tag + "cro", [1, 256], BF16).ap()
    l9sem = b.sem("l9")
    b.dma("q", lnl9[0:8, 0:32],
          inp["sta"].rearrange("(r q) e -> r q e", r=8)[:, 4096:4128, 0:1]
          .rearrange("r q one -> r (q one)"),
          deps=[(gsem, cc)], inc=l9sem, allow_slow_non_contiguous=True)
    b.dma("q", lnl9b[8:9, :], inp["negrow"], inc=l9sem)
    cB = b.op("v", lambda h: h.tensor_copy(lnl9b[0:8, :], lnl9[0:8, 0:32]), deps=[(l9sem, 32)])
    lcm = b.op("t", lambda h: h.matmul(psD[0:8, 256:288], mext, lnl9b,
                                       start=True, stop=True),
               deps=[(vsem, cB), (ssm, SM_ALL)], inc=tsem)
    ce = b.op("s", lambda h: h.activation(lnl9[0:8, 32:64], psD[0:8, 256:288], AF.Exp),
              deps=[(tsem, lcm)])
    cbf = b.op("v", lambda h: h.tensor_copy(lnl9b[0:8, :], lnl9[0:8, 32:64]),
               deps=[(ssem, ce)])
    crs = b.sem("cr")
    cd1 = b.dma("g", inp["crow"][0:1, :].rearrange("one (bb hh) -> (one bb) hh", bb=8),
                lnl9b[0:8, :], deps=[(vsem, cbf)], inc=crs)
    cd2 = b.dma("g", cro, inp["crow"], deps=[(crs, cd1)], inc=crs)
    cbm = b.op("t", lambda h: h.matmul(psD[:, 256:512], consts["ones1"], cro,
                                       start=True, stop=True),
               deps=[(crs, cd2)], inc=tsem)
    cbc = b.op("v", lambda h: h.tensor_copy(coefbc, psD[:, 256:512]), deps=[(tsem, cbm)])
    tbs = nc.alloc_sbuf_tensor_at(tag + "tbs", [128, 2 * 4096], F32, offset=scr_off).ap()
    tring = Ring(b, "tb", 2)
    cmb = cbc
    for bb in range(8):
        ltb = tring.dma(bb, "q",
                        tbs[:, (bb % 2) * 4096:(bb % 2 + 1) * 4096]
                        .rearrange("p (hh e) -> p hh e", hh=H),
                        inp["sta"].rearrange("(r q) e -> r q e", r=8)[bb, 0:4096, :]
                        .rearrange("(hh p) e -> p hh e", p=128),
                        deps=[(gsem, cc)] + ([(vsem, stup2[bb - 2])] if bb >= 2 else []))
        for h_ in range(H):
            op = ALU.add if bb > 0 else ALU.bypass
            if bb == 0:
                cmb = b.op("v", lambda h, bb=bb, h_=h_: h.tensor_scalar(
                    states[:, h_ * 128:(h_ + 1) * 128],
                    tbs[:, (bb % 2) * 4096 + h_ * 128:(bb % 2) * 4096 + (h_ + 1) * 128],
                    coefbc[:, bb * 32 + h_:bb * 32 + h_ + 1], None, op0=ALU.mult),
                    deps=[ltb, (vsem, cbc)])
            else:
                cmb = b.op("v", lambda h, bb=bb, h_=h_: h.scalar_tensor_tensor(
                    states[:, h_ * 128:(h_ + 1) * 128],
                    tbs[:, (bb % 2) * 4096 + h_ * 128:(bb % 2) * 4096 + (h_ + 1) * 128],
                    coefbc[:, bb * 32 + h_:bb * 32 + h_ + 1],
                    states[:, h_ * 128:(h_ + 1) * 128], op0=ALU.mult, op1=ALU.add),
                    deps=[ltb])
        if bb == 0:
            stup2 = {}
        stup2[bb] = cmb
    p3_done = cmb


    for h_ in range(H):
        kvh = h_ // 4
        qfc = frnt[h_]
        if h_ == 0:
            pre_chain(0, cl_h[0])
        if h_ + 1 < H:
            cl_h[h_ + 1] = clr.dma(h_ + 1, "q",
                                   cstl.rearrange("p (n e) -> p n e", n=NCH),
                                   cstd[h_ + 1].rearrange("n p e -> p n e"),
                                   deps=[(vsem, chn)])
        my_chn = chn
        g2 = None
        for n in range(NCH):
            b.op("v", lambda h, n=n, h_=h_: h.tensor_scalar(
                gr2hi[:, n * 128:(n + 1) * 128], ltc, ghi_tm(n)[:, h_:h_ + 1],
                None, op0=ALU.mult),
                deps=([(tsem, ym_prev)] if (n == 0 and ym_prev is not None) else ()))
            g2 = b.op("v", lambda h, n=n, h_=h_: h.tensor_scalar(
                gr2lo[:, n * 128:(n + 1) * 128], ltc, glo_tm(n)[:, h_:h_ + 1],
                None, op0=ALU.mult))
        d0g = []
        if wslot_guard[widx % 2] is not None:
            d0g.append((tsem, wslot_guard[widx % 2]))
        wb_g, cast_g = stream_w(widx, inp["wg"][h_], d0g, eng="v")
        sms = None
        for n in range(NCH):
            sms = b.op("t", lambda h, n=n, kvh=kvh: h.matmul(
                psA[:, n * 128:(n + 1) * 128],
                kfm[:, kvh * 1024 + n * 128:kvh * 1024 + (n + 1) * 128],
                qf[:, n * 128:(n + 1) * 128], start=True, stop=True),
                deps=[(vsem, qfc)] if n == 0 else (),
                inc=(tsem if n == NCH - 1 else None))
        gm = None
        for gi, grb in ((0, gr2hi), (1, gr2lo)):
            for half in range(2):
                gm = b.op("t", lambda h, gi=gi, grb=grb, half=half: h.matmul(
                    psB[:, half * 512:(half + 1) * 512], mgt,
                    grb[:, half * 512:(half + 1) * 512],
                    start=(gi == 0), stop=(gi == 1)),
                    deps=[(vsem, g2)] if (gi == 0 and half == 0) else (),
                    inc=(tsem if (gi == 1 and half == 1) else None))
        dse = None
        for n in range(NCH):
            dse = b.op("s", lambda h, n=n, h_=h_: h.activation(
                dsb[:, n * 128:(n + 1) * 128], psB[:, n * 128:(n + 1) * 128],
                AF.Exp, bias=lndt_tm(n)[:, h_:h_ + 1], scale=1.0),
                deps=[(tsem, gm)] if n == 0 else ())
        gmm0 = [cast_g]
        if sg_prev is not None:
            gmm0.append((ssem, sg_prev))
        lastg = None
        for ic in range(FC):
            for half in range(2):
                lastg = b.op("t", lambda h, ic=ic, half=half, wb=wb_g: h.matmul(
                    psC[:, half * 512:(half + 1) * 512],
                    wb[:, ic * 128:(ic + 1) * 128],
                    xbf[:, ic * T + half * 512:ic * T + (half + 1) * 512],
                    start=(ic == 0), stop=(ic == FC - 1)),
                    deps=gmm0 if ic == 0 else (),
                    inc=(tsem if (ic == FC - 1 and half == 1) else None))
        wslot_guard[widx % 2] = lastg; widx += 1
        dcb = b.op("v", lambda h: h.tensor_tensor(decb3, dsb3, utc_b, op=ALU.mult),
                   deps=[(ssem, dse)])
        pt = b.op("v", lambda h: h.tensor_tensor(PT2s, psA, decb, op=ALU.mult),
                  deps=[(tsem, sms)])
        pt_prev = pt
        ym = None
        for n in range(NCH):
            ym = b.op("t", lambda h, n=n, kvh=kvh: h.matmul(
                psD[:, n * 128:(n + 1) * 128], PT2s[:, n * 128:(n + 1) * 128],
                vtm[:, kvh * 1024 + n * 128:kvh * 1024 + (n + 1) * 128],
                start=True, stop=True),
                deps=[(vsem, pt)] if n == 0 else (),
                inc=(tsem if n == NCH - 1 else None))
        ym_prev = ym
        y2 = None
        for n in range(NCH):
            y2 = b.op("t", lambda h, n=n: h.matmul(
                psB[:, n * 128:(n + 1) * 128], qf[:, n * 128:(n + 1) * 128],
                prevball[:, n * 128:(n + 1) * 128], start=True, stop=True),
                deps=[(ssem, dse), (vsem, my_chn)] if n == 0 else (),
                inc=(tsem if n == NCH - 1 else None))
        y2_last = y2
        if h_ + 1 < H:
            front_a(h_ + 1)
        s1 = None
        for n in range(NCH):
            s1 = b.op("v", lambda h, n=n, h_=h_: h.tensor_scalar(
                dsb[:, n * 128:(n + 1) * 128], psB[:, n * 128:(n + 1) * 128],
                expG_tm(n)[:, h_:h_ + 1], None, op0=ALU.mult),
                deps=[(tsem, y2)] if n == 0 else ())
        ytd = b.op("v", lambda h: h.tensor_tensor(yt, psD, dsb, op=ALU.add),
                   deps=[(tsem, ym)])
        if DBG4 and h_ == 0:
            dump4(inp["dbg"][576:704, :], yt, deps=[(vsem, ytd)])
            dump4(inp["dbg"][704:832, :], kfm[:, 0:1024])
            dump4(inp["dbg"][832:960, :], qf)
        b.op("v", lambda h: h.tensor_tensor(dsb, yt, yt, op=ALU.mult), deps=[(vsem, ytd)])
        b.op("v", lambda h: h.tensor_reduce(
            rstd8, dsb3, axis=AX.X, op=ALU.add))
        mrs = b.op("v", lambda h: h.tensor_scalar(rstd8, rstd8, 1.0 / 128.0, 1e-5,
                                                  op0=ALU.mult, op1=ALU.add))
        b.op("s", lambda h: h.activation(rstd8, rstd8, AF.Ln), deps=[(vsem, mrs)])
        rs = b.op("s", lambda h: h.activation(rstd8, rstd8, AF.Exp, scale=-0.5))
        ysc = None
        for n in range(NCH):
            ysc = b.op("v", lambda h, n=n: h.tensor_scalar(
                yt[:, n * 128:(n + 1) * 128], yt[:, n * 128:(n + 1) * 128],
                rstd8[:, n:n + 1], None, op0=ALU.mult),
                deps=[(ssem, rs)] if n == 0 else ())
        sgdeps = [(tsem, lastg)]
        if h_ > 0:
            sgdeps.append(ystore[h_ - 1])
        sg = b.op("s", lambda h: h.activation(gsil, psC[:, :], AF.Silu), deps=sgdeps)
        sg_prev = sg
        gt = None
        for n in range(NCH):
            gt = b.op("t", lambda h, n=n: h.transpose(
                psT2a[:, n * 128:(n + 1) * 128], gsil[:, n * 128:(n + 1) * 128],
                consts["identB"]),
                deps=[(ssem, sg), (vsem, ytd)] if n == 0 else (),
                inc=(tsem if n == NCH - 1 else None))
        gtc = b.op("v", lambda h: h.tensor_copy(qt, psT2a), deps=[(tsem, gt)])
        ytt = b.op("v", lambda h: h.tensor_tensor(ytg, yt, qt, op=ALU.mult))
        ty = None
        for n in range(NCH):
            ty = b.op("t", lambda h, n=n: h.transpose(
                psT2a[:, n * 128:(n + 1) * 128], ytg[:, n * 128:(n + 1) * 128],
                consts["identB"]),
                deps=[(vsem, ytt)] if n == 0 else (),
                inc=(tsem if n == NCH - 1 else None))
        ycopy = b.op("v", lambda h: h.tensor_scalar(
            gsil, psT2a, gnormc, None, op0=ALU.mult), deps=[(tsem, ty)])
        ystore[h_] = ysr.dma(h_, "q", ydram[h_], gsil, deps=[(vsem, ycopy)])
        if h_ + 1 < H:
            front_b(h_ + 1)
            pre_chain(h_ + 1, cl_h[h_ + 1])
    p4_last = ycopy

    xs = nc.alloc_sbuf_tensor_at(tag + "xs", [128, 2 * 1024], F32, offset=scr_off).ap()
    yo = nc.alloc_sbuf_tensor_at(tag + "yo", [128, 2 * 1024], F32, offset=scr_off + 8192).ap()
    ysb2 = nc.alloc_sbuf_tensor_at(tag + "ysb2", [128, 2 * 1024], BF16, offset=scr_off + 16384).ap()
    wosg = nc.alloc_sbuf_tensor_at(tag + "wosg", [128, 2048], F32, offset=scr_off + 20480).ap()
    wob3 = nc.alloc_sbuf_tensor_at(tag + "wob3", [128, 2 * 4096], BF16, offset=ktm_off).ap()
    wob3b = nc.alloc_sbuf_tensor_at(tag + "wob3b", [128, 2 * 4096], BF16,
                                    offset=scr_off + 28672).ap()
    ylr = Ring(b, "yl", 2)
    xr2 = Ring(b, "x2", 2)
    so2 = Ring(b, "s2", 2)
    wor2 = Ring(b, "w2", 2)
    groups = [list(range(g, min(g + 4, FC))) for g in range(0, FC, 4)]
    p4_all = p4_last
    ev5 = {}; st5 = {}; mm5 = {}
    wcast5 = {"last": None}
    gi = 0
    lastm2 = None
    for grp in groups:
        for j, oc in enumerate(grp):
            wdeps = [(vsem, p4_all)]
            if gi >= 1 and lastm2 is not None:
                wdeps.append((tsem, lastm2[H - 1]))
            for hh in range(2):
                lwdeps = list(wdeps)
                if wcast5.get("last") is not None:
                    lwdeps.append((vsem, wcast5["last"]))
                lwo = wor2.dma(2 * oc + hh, "q",
                               wosg.rearrange("p (i o) -> p i o", i=16),
                               inp["wo"][oc, hh * 16:(hh + 1) * 16].rearrange("i p o -> p i o"),
                               deps=lwdeps)
                wdst = wob3 if j < 2 else wob3b
                jj = j % 2
                wcast5[(oc, hh)] = wcast5["last"] = b.op(
                    "v", lambda h, jj=jj, hh=hh, wdst=wdst: h.tensor_copy(
                        wdst[:, (jj * 2 + hh) * 2048:(jj * 2 + hh + 1) * 2048],
                        wosg), deps=[lwo])
        pss = [psA, psB, psC, psD][:len(grp)]
        lastm = {}
        for ic in range(H):
            yldeps = [ystore[ic]]
            if ic >= 2:
                yldeps.append((tsem, lastm[ic - 2]))
            elif lastm2 is not None:
                yldeps.append((tsem, lastm2[H - 1]))
            yl = ylr.dma(ic, "q", ysb2[:, (ic % 2) * 1024:(ic % 2 + 1) * 1024],
                         ydram[ic], deps=yldeps)
            for j, oc in enumerate(grp):
                wdst = wob3 if j < 2 else wob3b
                jj = j % 2
                mdeps = [yl]
                if ic == 0:
                    mdeps.append((vsem, wcast5[(oc, 1)]))
                    if oc - 4 in ev5:
                        mdeps.append((vsem, ev5[oc - 4]))
                for half in range(2):
                    lastm[ic] = b.op("t", lambda h, jj=jj, ic=ic, half=half, j=j, wdst=wdst: h.matmul(
                        pss[j][:, half * 512:(half + 1) * 512],
                        wdst[:, jj * 4096 + ic * 128:(jj * 4096) + (ic + 1) * 128],
                        ysb2[:, (ic % 2) * 1024 + half * 512:(ic % 2) * 1024 + (half + 1) * 512],
                        start=(ic == 0), stop=(ic == H - 1)),
                        deps=mdeps if half == 0 else (),
                        inc=(tsem if half == 1 else None))
        lastm2 = lastm
        for j, oc in enumerate(grp):
            ldx = xr2.dma(oc, "q", xs[:, (oc % 2) * 1024:(oc % 2 + 1) * 1024],
                          x_in[oc * 128:(oc + 1) * 128, :],
                          deps=[(vsem, ev5[oc - 2])] if oc - 2 in ev5 else [(vsem, p4_all)])
            ev5[oc] = b.op("v", lambda h, j=j, oc=oc: h.tensor_tensor(
                yo[:, (oc % 2) * 1024:(oc % 2 + 1) * 1024], pss[j][:, :],
                xs[:, (oc % 2) * 1024:(oc % 2 + 1) * 1024], op=ALU.add),
                deps=[(tsem, lastm[H - 1]), ldx] + ([st5[oc - 2]] if oc - 2 in st5 else []))
            st5[oc] = so2.dma(oc, "q", x_out[oc * 128:(oc + 1) * 128, :],
                              yo[:, (oc % 2) * 1024:(oc % 2 + 1) * 1024],
                              deps=[(vsem, ev5[oc])])
        gi += 1
    for s in so2.sems:
        b.wait("q", s, s.n)
    return st5



def sbuf_mark(nc):
    return (nc.sbuf_base, nc.psum_base)


def sbuf_rollback(nc, mark):
    cur = nc.sbuf_base
    if cur > mark[0]:
        nc.bump_sbuf(-(cur - mark[0]))
    curp = nc.psum_base
    if curp > mark[1]:
        nc.bump_psum(banks=-(curp - mark[1]))


def build_fused(nc, inmaps_extra):
    xA = nc.dram_tensor("xA", [D, T], F32, kind="ExternalInput").ap()
    xB = nc.dram_tensor("xB", [D, T], F32).ap()
    xC = nc.dram_tensor("xC", [D, T], F32).ap()
    x_out = nc.dram_tensor("x_out", [D, T], F32, kind="ExternalOutput").ap()

    consts_d = make_consts(nc)
    b = B(nc)
    csb, _ = load_consts(b, consts_d)

    p0w_d = nc.dram_tensor("p0w", [FC, 128, 68], F32, kind="ExternalInput").ap()
    bw_d = nc.dram_tensor("bw", [FC, 64, 128], F32, kind="ExternalInput").ap()
    wqkv_d = nc.dram_tensor("wqkv", [12, 32, 128, 128], F32, kind="ExternalInput").ap()
    bqkv_d = nc.dram_tensor("bqkv", [128, 12], F32, kind="ExternalInput").ap()
    wo0_d = nc.dram_tensor("wo0", [32, 8, 64, 128], F32, kind="ExternalInput").ap()
    wob0_d = nc.dram_tensor("wob0", [128, 32], F32, kind="ExternalInput").ap()
    pk_d = nc.dram_tensor("pkt", [2048, 4096], I32, kind="ExternalInput").ap()
    or_d = nc.dram_tensor("orthot", [4096, 4096], F32, kind="ExternalInput").ap()
    sc_d = nc.dram_tensor("scales_r", [1, 4096], F32, kind="ExternalInput").ap()
    l1names = dict(wq=[32, 32, 128, 128], wk=[8, 32, 128, 128], wv=[8, 32, 128, 128],
                   wg=[32, 32, 128, 128], wo=[32, 32, 128, 128], win=[32, 128, 32],
                   dtb=[32, 1], alog=[32, 1], gnorm=[128, 1])
    l1dr = {k: nc.dram_tensor(k, s, F32, kind="ExternalInput").ap() for k, s in l1names.items()}
    l1bn = dict(ropec=[128, 1024], ropes=[128, 1024], maskgt=[128, 128],
                ltc=[128, 128], utc=[128, 128], mext=[9, 8], negrow=[1, 32])
    for k, s in l1bn.items():
        l1dr[k] = nc.dram_tensor(k, s, BF16, kind="ExternalInput").ap()
    kvc = nc.dram_tensor("kvc", [1280, 1024], BF16).ap()
    kva = nc.dram_tensor("kva", [10240, 1024], BF16).ap()
    mrow = nc.dram_tensor("mrow", [2, 1024], BF16).ap()
    wt_d = nc.dram_tensor("wt", [32, 32, 128, 128], BF16).ap()
    l1dr["cstd"] = nc.dram_tensor("cstd", [32, 8, 128, 128], F32).ap()
    l1dr["stc"] = nc.dram_tensor("stc", [4128, 128], F32).ap()
    l1dr["sta"] = nc.dram_tensor("sta", [8 * 4128, 128], F32).ap()
    l1dr["crow"] = nc.dram_tensor("crow", [1, 256], BF16).ap()
    l1dr["ydram"] = nc.dram_tensor("ydram", [32, 128, 1024], BF16).ap()
    l1dr["dbg"] = None
    l1dr["consts"] = csb

    mark = sbuf_mark(nc)
    build_l0_ffn(b, xA, xB, dict(p0w=p0w_d, bw=bw_d, consts=csb,
                                 identF=csb["identF"], identF64=csb["identF64"]), "f0a")
    b.fence(); sbuf_rollback(nc, mark)
    build_l0_attn(b, xB, xC, dict(wqkv=wqkv_d, bqkv=bqkv_d, wo=wo0_d, wob=wob0_d,
                                  consts=csb, kvc=kvc, kva=kva, mrow=mrow,
                                  c_ones1=consts_d["ones1"], dbg=None), "a0")
    b.fence(); sbuf_rollback(nc, mark)
    build_l0_ffn(b, xC, xB, dict(p0w=p0w_d, bw=bw_d, consts=csb,
                                 identF=csb["identF"], identF64=csb["identF64"]), "f0b")
    b.fence(); sbuf_rollback(nc, mark)
    build_l1_ffn(b, xB, xC, dict(pkt=pk_d, orthot=or_d, scales_r=sc_d, wt=wt_d,
                                 consts=csb, xn_out=None), "q1a", mode="build")
    b.fence(); sbuf_rollback(nc, mark)
    build_l1_attn(b, xC, xB, l1dr, "g1")
    b.fence(); sbuf_rollback(nc, mark)
    build_l1_ffn(b, xB, x_out, dict(wt=wt_d, consts=csb, xn_out=None), "q1b", mode="use")
    b.finalize()
    return nc


def fused_kernel(**inputs):
    x = np.ascontiguousarray(np.asarray(inputs["L0_attn_x"], np.float32)[0])
    hc = host_consts()
    order = ORDER

    p0w, bw = host_l0_ffn_inputs(np.asarray(inputs["L0_ffn_A"], np.float32)[:, :, order],
                                 np.asarray(inputs["L0_ffn_B"], np.float32)[:, order, :],
                                 np.asarray(inputs["L0_ffn_Wr"], np.float32)[:, order])
    wqkv, bqkv, wo0, wob0 = host_l0_attn_inputs(
        np.asarray(inputs["L0_attn_Wqk_w"], np.float32),
        np.asarray(inputs["L0_attn_Wqk_b"], np.float32),
        np.asarray(inputs["L0_attn_Wv_w"], np.float32),
        np.asarray(inputs["L0_attn_Wv_b"], np.float32),
        np.asarray(inputs["L0_attn_Wo_w"], np.float32),
        np.asarray(inputs["L0_attn_Wo_b"], np.float32))
    pk = np.asarray(inputs["L1_ffn_base_packed"], np.int32)
    scales = np.asarray(inputs["L1_ffn_scales"], np.float32)
    ortho = np.asarray(inputs["L1_ffn_w_ortho"], np.float32)
    pkt = np.ascontiguousarray(pk[order, :].T)
    orthot = np.ascontiguousarray(ortho[np.ix_(order, order)].T)
    scales_r = np.ascontiguousarray(scales[order].reshape(1, 4096))

    ncb = bass.Bass("TRN2", target_bir_lowering=False, debug=False)
    build_fused(ncb, None)

    in_maps = []
    for c in range(NCORE):
        hi = host_l1_attn_inputs(
            np.asarray(inputs["L1_attn_Wq"], np.float32),
            np.asarray(inputs["L1_attn_Wk"], np.float32),
            np.asarray(inputs["L1_attn_Wv"], np.float32),
            np.asarray(inputs["L1_attn_Wo"], np.float32),
            np.asarray(inputs["L1_attn_Wg"], np.float32),
            np.asarray(inputs["L1_attn_Win"], np.float32),
            np.asarray(inputs["L1_attn_dt_bias"], np.float32),
            np.asarray(inputs["L1_attn_A_log_bias"], np.float32),
            np.asarray(inputs["L1_attn_gnorm_w"], np.float32), c)
        m = {"xA": fm_of(x[c * T:(c + 1) * T][:, order]), **hc,
             "p0w": p0w, "bw": bw, "wqkv": wqkv, "bqkv": bqkv, "wo0": wo0, "wob0": wob0,
             "pkt": pkt, "orthot": orthot, "scales_r": scales_r,
             "negrow": np.full((1, 32), -1e30, BF)}
        for k in ["wq", "wk", "wv", "wg", "wo", "win", "dtb", "alog", "gnorm",
                  "ropec", "ropes", "maskgt", "ltc", "utc", "mext"]:
            m[k] = hi[k]
        in_maps.append(m)
    res = run_bass_kernel_spmd(ncb, in_maps, core_ids=list(range(NCORE)))
    out = np.concatenate([np.asarray(r["x_out"], np.float32).T for r in res.results], axis=0)
    out = out[:, INV_ORDER]
    return out[None, :, :].astype(np.float32)


def kernel(**inputs):
    return fused_kernel(**inputs)
